# version 16
# baseline (speedup 1.0000x reference)
from contextlib import ExitStack

import numpy as np
import ml_dtypes

import concourse.bass as bass
import concourse.mybir as mybir
from concourse.bass_utils import run_bass_kernel_spmd

F32 = mybir.dt.float32
BF16 = mybir.dt.bfloat16
AF = mybir.ActivationFunctionType
ALU = mybir.AluOpType
AX = mybir.AxisListType

NCORES = 8
D = 1024
S = 2048
NSEQ = 2
NT = S // 128
QK = 16
EPS = 1e-5



ENGS = ("sync", "gpsimd", "tensor", "vector", "scalar")


class Sem:
    def __init__(self, nc, ctx, name):
        self.h = ctx.enter_context(nc.semaphore(name))
        self.val = 0


class Phase:

    def __init__(self, name):
        self.name = name
        self.steps = {e: [] for e in ENGS}

    def add(self, eng, emit, waits=(), incs=()):
        self.steps[eng].append((tuple(waits), emit, tuple(incs)))

    def emit(self, nc):
        with nc.Block() as blk:
            for eng in ENGS:
                steps = self.steps[eng]
                if not steps:
                    continue

                def body(q, steps=steps):
                    for waits, emit, incs in steps:
                        for (sem, v) in waits:
                            if v > 0:
                                q.wait_ge(sem.h, v)
                        ins = emit(q)
                        for (sem, n) in incs:
                            ins.then_inc(sem.h, n)

                getattr(blk, eng)(body)


class Sched:

    def __init__(self, nc, ctx, prefix):
        self.nc = nc
        self.prefix = prefix
        self.ph = None
        self.sem = {e: Sem(nc, ctx, f"{prefix}_{e}") for e in ENGS}
        self.dmaq = {e: [Sem(nc, ctx, f"{prefix}_dma_{e}{i}") for i in range(4)]
                     for e in ("gpsimd", "sync", "scalar")}
        self.dmac = {e: 0 for e in ("gpsimd", "sync", "scalar")}
        self.doutq = [Sem(nc, ctx, f"{prefix}_dout{i}") for i in range(4)]
        self.doutc = 0

    def phase(self, name):
        assert self.ph is None, "previous phase not emitted"
        self.ph = Phase(f"{self.prefix}_{name}")
        return self.ph

    def emit_ph(self):
        self.ph.emit(self.nc)
        self.ph = None

    def op(self, eng, emit, waits=()):
        s = self.sem[eng]
        w = [(s, s.val)] + [x for x in waits if x is not None]
        s.val += 1
        self.ph.add(eng, emit, w, [(s, 1)])
        return (s, s.val)

    def tmm(self, emit, waits=(), mark=False):
        incs = []
        ret = None
        if mark:
            s = self.sem["tensor"]
            s.val += 1
            incs = [(s, 1)]
            ret = (s, s.val)
        self.ph.add("tensor", emit, [x for x in waits if x is not None], incs)
        return ret

    def dma_in(self, eng, out, in_, waits=()):
        sem = self.dmaq[eng][self.dmac[eng] % 4]
        self.dmac[eng] += 1
        w = [(sem, sem.val)] + [x for x in waits if x is not None]
        sem.val += 16
        self.ph.add(eng, lambda q, o=out, i=in_: q.dma_start(out=o, in_=i),
                    w, [(sem, 16)])
        return (sem, sem.val)

    def dma_marks(self):
        return [(s, s.val) for ring in self.dmaq.values() for s in ring if s.val > 0]

    def dma_out(self, eng, out, in_, waits=()):
        sem = self.doutq[self.doutc % 4]
        self.doutc += 1
        w = [(sem, sem.val)] + [x for x in waits if x is not None]
        sem.val += 16
        self.ph.add(eng, lambda q, o=out, i=in_: q.dma_start(out=o, in_=i),
                    w, [(sem, 16)])
        return (sem, sem.val)

    def drain_outs(self):
        w = [(x, x.val) for x in self.doutq if x.val > 0]
        if w:
            self.ph.add("sync", lambda q: q.nop(), w, ())

    def finalize(self):
        fin = Phase(f"{self.prefix}_fin")
        fin.add("sync", lambda q: q.nop(),
                [(s, s.val) for s in self.doutq if s.val > 0], ())
        fin.emit(self.nc)


def col_chunks(n, c=512):
    out = []
    s = 0
    while s < n:
        w = min(c, n - s)
        out.append((s, w))
        s += w
    return out



def emit_l0_attn(sc: Sched, E, io):
    nc = sc.nc
    x, wq, wk, bq, bk = io["x"], io["wq"], io["wk"], io["bq"], io["bk"]
    maskq, maskT, ident, ones1, h1 = (io["maskq"], io["maskT"], io["ident"],
                                      io["ones1"], io.get("h1"))
    R = io["R"]
    id_s = io["id_s"]
    inter = io.get("write_inter", True)

    bq_s = E(nc.sbuf_tensor("bq_s", [QK, 1], F32))
    bk_s = E(nc.sbuf_tensor("bk_s", [QK, 1], F32))
    mq_s = E(nc.sbuf_tensor("mq_s", [128, 128], F32))
    mT_s = E(nc.sbuf_tensor("mT_s", [128, 128], F32))
    on_s = E(nc.sbuf_tensor("on_s", [1, 128], BF16))
    wq_s = E(nc.sbuf_tensor("wq_s", [QK, D], BF16))
    wk_s = E(nc.sbuf_tensor("wk_s", [QK, D], BF16))
    wqT = E(nc.sbuf_tensor("wqT", [128, 8, QK], BF16))
    wkT = E(nc.sbuf_tensor("wkT", [128, 8, QK], BF16))

    d_o = {}
    with ExitStack() as sctx:
        SE = sctx.enter_context
        qT = SE(nc.sbuf_tensor("qT", [QK, S], BF16))
        kT = SE(nc.sbuf_tensor("kT", [QK, S], BF16))
        xT = SE(nc.sbuf_tensor("xT", [128, 2, 8, 128], BF16))
        m4row = SE(nc.sbuf_tensor("m4row", [1, S], BF16))
        denom = SE(nc.sbuf_tensor("denom", [128, NT], F32))
        mx = SE(nc.sbuf_tensor("mx", [128, 1], F32))
        m4b = SE(nc.sbuf_tensor("m4b", [128, 1], BF16))
        bias1 = SE(nc.sbuf_tensor("bias1", [128, 1], F32))
        dump = SE(nc.sbuf_tensor("dump", [128, S], BF16))
        prob = SE(nc.sbuf_tensor("prob", [128, NT, 128], BF16))
        rec = SE(nc.sbuf_tensor("rec", [128, 1], F32))
        SF = SE(nc.sbuf_tensor("SF", [128, D], F32))
        bn6 = SE(nc.sbuf_tensor("bn6", [128, 12], F32))
        mv = SE(nc.sbuf_tensor("mv", [128, 2], F32))
        veps = SE(nc.sbuf_tensor("veps", [128, 1], F32))
        sd = SE(nc.sbuf_tensor("sd", [128, 1], F32))
        rstd = SE(nc.sbuf_tensor("rstd", [128, 1], F32))
        mnra = SE(nc.sbuf_tensor("mnra", [128, 1], F32))
        OUT = SE(nc.sbuf_tensor("OUT", [128, 2, D], F32))

        for seq in range(NSEQ):
            Roff = seq * NT

            with ExitStack() as pctx:
                PE = pctx.enter_context
                PT = [PE(nc.psum_tensor(f"PT{i}_{seq}", [128, D], BF16)) for i in range(2)]
                QP = [PE(nc.psum_tensor(f"QP{i}_{seq}", [QK, 128], F32)) for i in range(2)]
                KP = [PE(nc.psum_tensor(f"KP{i}_{seq}", [QK, 128], F32)) for i in range(2)]
                WTP = PE(nc.psum_tensor(f"WTP_{seq}", [128, 2 * QK], BF16))

                sc.phase(f"s{seq}_p1")
                sc.dma_in("gpsimd", R[:, Roff:Roff + NT, :],
                          x[seq].rearrange("(t p) d -> p t d", p=128))
                if seq == 0:
                    for src, dst in ((wq, wq_s), (wk, wk_s)):
                        sc.dma_in("gpsimd", dst[:], src[:])
                    for src, dst in ((ident, id_s), (ones1, on_s), (bq, bq_s),
                                     (bk, bk_s), (maskq, mq_s), (maskT, mT_s)):
                        sc.dma_in("sync", dst[:], src[:])
                dma_all = sc.dma_marks()

                if seq == 0:
                    last_ev = None
                    for c in range(8):
                        sc.tmm(lambda q, c=c: q.transpose(
                            WTP[:, 0:QK], wq_s[:, c * 128:(c + 1) * 128],
                            id_s[0:QK, 0:QK]),
                            waits=(dma_all if c == 0 else []) + [last_ev])
                        m = sc.tmm(lambda q, c=c: q.transpose(
                            WTP[:, QK:2 * QK], wk_s[:, c * 128:(c + 1) * 128],
                            id_s[0:QK, 0:QK]),
                            mark=True)
                        sc.op("scalar", lambda q, c=c: q.copy(wqT[:, c, :], WTP[:, 0:QK]),
                              waits=[m])
                        last_ev = sc.op("scalar", lambda q, c=c: q.copy(
                            wkT[:, c, :], WTP[:, QK:2 * QK]))

                v_x = [None] * NT
                s_qk = [None] * NT
                for i in range(NT):
                    par = i % 2
                    for c in range(8):
                        w0 = []
                        if c == 0:
                            w0 = (dma_all if i == 0 else []) + [
                                  v_x[i - 2] if i >= 2 else (last_ev if (seq == 0 and i == 0) else None)]
                        m_tp = sc.tmm(lambda q, i=i, c=c, par=par: q.transpose(
                            PT[par][:, c * 128:(c + 1) * 128],
                            R[:, Roff + i, c * 128:(c + 1) * 128], id_s[:]), waits=w0,
                            mark=(c == 7))
                    v_x[i] = sc.op("vector", lambda q, par=par: q.tensor_copy(
                        xT[:, par, :, :].rearrange("p a b -> p (a b)"), PT[par][:]),
                        waits=[m_tp])
                    for c in range(8):
                        sc.tmm(lambda q, c=c, par=par: q.matmul(
                            QP[par][:], wqT[:, c, :], xT[:, par, c, :],
                            start=(c == 0), stop=(c == 7)),
                            waits=[v_x[i] if c == 0 else None,
                                   s_qk[i - 2] if (c == 0 and i >= 2) else None])
                    for c in range(8):
                        t_qk = sc.tmm(lambda q, c=c, par=par: q.matmul(
                            KP[par][:], wkT[:, c, :], xT[:, par, c, :],
                            start=(c == 0), stop=(c == 7)), mark=(c == 7))
                    sc.op("scalar", lambda q, i=i, par=par: q.activation(
                        qT[:, i * 128:(i + 1) * 128], QP[par][:], AF.Identity, bias=bq_s[:]),
                        waits=[t_qk])
                    s_qk[i] = sc.op("scalar", lambda q, i=i, par=par: q.activation(
                        kT[:, i * 128:(i + 1) * 128], KP[par][:], AF.Identity, bias=bk_s[:]))
                sc.emit_ph()

            with ExitStack() as pctx:
                PE = pctx.enter_context
                SC = PE(nc.psum_tensor(f"SC_{seq}", [128, S], F32))
                MR = PE(nc.psum_tensor(f"MR_{seq}", [1, 128], BF16))

                sc.phase(f"s{seq}_p2")
                s_p2 = [None] * NT
                s_mr = [None] * NT
                m_mr = [None] * NT
                for qi in range(NT):
                    ncols = (qi + 1) * 128
                    chunks = col_chunks(ncols)
                    for j, (st, w) in enumerate(chunks):
                        t_sc = sc.tmm(lambda q, qi=qi, st=st, w=w: q.matmul(
                            SC[:, st:st + w], qT[:, qi * 128:(qi + 1) * 128],
                            kT[:, st:st + w], start=True, stop=True),
                            waits=[s_p2[qi - 1] if (qi >= 1 and st == 0) else None],
                            mark=(j == len(chunks) - 1))
                    sc.op("vector", lambda q, qi=qi: q.tensor_tensor(
                        SC[:, qi * 128:(qi + 1) * 128], SC[:, qi * 128:(qi + 1) * 128],
                        mq_s[:], ALU.add),
                        waits=[t_sc, s_p2[qi - 1] if qi >= 1 else None])
                    sc.op("vector", lambda q, ncols=ncols: q.reduce_max(
                        mx[:], SC[:, 0:ncols], axis=AX.X))
                    sc.op("vector", lambda q: q.tensor_scalar(
                        m4b[:], mx[:], -1.0, None, ALU.mult),
                        waits=[m_mr[qi - 1] if qi >= 1 else None])
                    v_p2 = sc.op("vector", lambda q: q.tensor_scalar(
                        bias1[:], m4b[:], 0.25, None, ALU.mult))
                    s_p2[qi] = sc.op("scalar", lambda q, qi=qi, ncols=ncols: q.activation(
                        dump[:, 0:ncols], SC[:, 0:ncols], AF.Exp, scale=0.25,
                        bias=bias1[:], accum_out=denom[:, qi:qi + 1]), waits=[v_p2])
                    m_mr[qi] = sc.tmm(lambda q: q.transpose(MR[:], m4b[:], id_s[:]),
                                      waits=[v_p2, s_mr[qi - 1] if qi >= 1 else None],
                                      mark=True)
                    s_mr[qi] = sc.op("scalar", lambda q, qi=qi: q.copy(
                        m4row[0:1, qi * 128:(qi + 1) * 128], MR[:]), waits=[m_mr[qi]])
                sc.emit_ph()

            with ExitStack() as pctx:
                PE = pctx.enter_context
                ST = [PE(nc.psum_tensor(f"STp{i}_{seq}", [128, 128], F32)) for i in range(4)]
                PV = PE(nc.psum_tensor(f"PV_{seq}", [128, D], F32))

                sc.phase(f"s{seq}_p3")
                s_exp = {}
                v_evac = {}
                pv_marks = {}
                steps = [(qi, ki) for qi in range(NT - 1, -1, -1) for ki in range(qi + 1)]
                sidx = {sk: i for i, sk in enumerate(steps)}
                DEPTH = 2

                def emit_st(idx):
                    qi, ki = steps[idx]
                    par = idx % 4
                    st_guard = s_exp.get(idx - 4)
                    sc.tmm(lambda q, qi=qi, ki=ki, par=par: q.matmul(
                        ST[par][:], kT[:, ki * 128:(ki + 1) * 128],
                        qT[:, qi * 128:(qi + 1) * 128], start=True, stop=False),
                        waits=[st_guard])
                    t_st = sc.tmm(lambda q, qi=qi, par=par: q.matmul(
                        ST[par][:], on_s[:], m4row[0:1, qi * 128:(qi + 1) * 128],
                        start=False, stop=True), mark=True)
                    wmask = None
                    if ki == qi:
                        wmask = sc.op("vector", lambda q, par=par: q.tensor_tensor(
                            ST[par][:], ST[par][:], mT_s[:], ALU.add), waits=[t_st])
                    pguard = pv_marks.get((qi + 1, ki))
                    s_exp[idx] = sc.op("scalar", lambda q, ki=ki, par=par: q.activation(
                        prob[:, ki, :], ST[par][:], AF.Exp, scale=0.25),
                        waits=[t_st, wmask, pguard])

                def emit_pv(idx):
                    qi, ki = steps[idx]
                    for (st, w) in ((0, 512), (512, 512)):
                        pv_marks[(qi, ki)] = sc.tmm(lambda q, qi=qi, ki=ki, st=st, w=w: q.matmul(
                            PV[:, st:st + w], prob[:, ki, :],
                            R[:, Roff + ki, st:st + w],
                            start=(ki == 0), stop=(ki == qi)),
                            waits=[s_exp[idx] if st == 0 else None,
                                   v_evac.get(qi + 1) if (ki == 0 and st == 0) else None],
                            mark=(st == 512))
                    if ki == qi:
                        sc.op("vector", lambda q, qi=qi: q.reciprocal(
                            rec[:], denom[:, qi:qi + 1]), waits=[pv_marks[(qi, qi)]])
                        sc.op("vector", lambda q, qi=qi: q.scalar_tensor_tensor(
                            SF[:], PV[:], rec[:], R[:, Roff + qi, :], ALU.mult, ALU.add))
                        v_evac[qi] = sc.op("vector", lambda q, qi=qi: q.tensor_copy(
                            R[:, Roff + qi, :], SF[:]))

                for idx in range(len(steps) + DEPTH):
                    if idx - DEPTH >= 0:
                        emit_pv(idx - DEPTH)
                    if idx < len(steps):
                        emit_st(idx)
                sc.emit_ph()

            sc.phase(f"s{seq}_p4")
            s_sd = [None] * NT
            v_rec = [None] * NT
            g_nrm = [None] * NT
            for i in range(NT):
                gi = seq * NT + i
                par = i % 2
                sc.op("vector", lambda q, i=i: q.bn_stats(bn6[:, 0:6], R[:, Roff + i, 0:512]),
                      waits=[s_sd[i - 1] if i >= 1 else None,
                             g_nrm[i - 1] if i >= 1 else None])
                sc.op("vector", lambda q, i=i: q.bn_stats(bn6[:, 6:12], R[:, Roff + i, 512:1024]))
                v_st = sc.op("vector", lambda q: q.bn_aggr(mv[:], bn6[:]))
                s_sd[i] = sc.op("scalar", lambda q: q.activation(
                    sd[:], mv[:, 1:2], AF.Sqrt, bias=io["eps_t"][:]),
                    waits=[v_st, v_rec[i - 1] if i >= 1 else None])
                v_rec[i] = sc.op("vector", lambda q: q.reciprocal(rstd[:], sd[:]),
                                 waits=[s_sd[i]])
                if inter:
                    sc.op("vector", lambda q, i=i, par=par: q.tensor_scalar(
                        OUT[:, par, :], R[:, Roff + i, :], mv[:, 0:1], rstd[:],
                        ALU.subtract, ALU.mult),
                        waits=[d_o.get(gi - 2)])
                    g_nrm[i] = sc.op("vector", lambda q, i=i, par=par: q.tensor_copy(
                        R[:, Roff + i, :], OUT[:, par, :]))
                    d_o[gi] = sc.dma_out("sync", h1[seq, i * 128:(i + 1) * 128, :],
                                         OUT[:, par, :], waits=[g_nrm[i]])
                else:
                    v_mnr = sc.op("vector", lambda q: q.tensor_scalar(
                        mnra[:], mv[:, 0:1], rstd[:], -1.0, ALU.mult, ALU.mult))
                    g_nrm[i] = sc.op("gpsimd", lambda q, i=i: q.tensor_scalar(
                        R[:, Roff + i, :], R[:, Roff + i, :], rstd[:], mnra[:],
                        ALU.mult, ALU.add), waits=[v_mnr])
            if seq == NSEQ - 1:
                sc.drain_outs()
            sc.emit_ph()




def emit_load_R(sc: Sched, io, src_ap):
    sc.phase("loadR")
    R = io["R"]
    for seq in range(NSEQ):
        sc.dma_in("gpsimd", R[:, seq * NT:(seq + 1) * NT, :],
                  src_ap[seq].rearrange("(t p) d -> p t d", p=128))
    tok = sc.dma_marks()
    sc.ph.add("gpsimd", lambda q: q.nop(), tok, ())
    sc.emit_ph()
    return tok


def emit_l0_ffn(sc: Sched, E, io):
    nc = sc.nc
    R, id_s = io["R"], io["id_s"]
    h2 = io.get("h2")
    inter = io.get("write_inter", True)
    ones_f = io["ones_fs"]

    bctx = ExitStack()
    E = bctx.enter_context
    GV = E(nc.sbuf_tensor("GV", [128, 32, D], BF16))
    gb = E(nc.sbuf_tensor("gb", [128, 32], BF16))
    gbias = E(nc.sbuf_tensor("gbias", [128, 32], F32))
    bexp_pm = E(nc.sbuf_tensor("bexp_pm", [128, 32], F32))
    ubias_pm = E(nc.sbuf_tensor("ubias_pm", [128, 32], F32))
    bfv_u = E(nc.sbuf_tensor("bfv_u", [128, 5 * 32 * 2], F32))
    bfv_d = E(nc.sbuf_tensor("bfv_d", [128, 5 * 32 * 2], F32))
    bfvb_u = E(nc.sbuf_tensor("bfvb_u", [128, 5, 4, 16], F32))
    bias_tmp = E(nc.sbuf_tensor("bias_tmp", [128, 64], F32))
    bias_bc = E(nc.sbuf_tensor("bias_bc", [128, D], F32))

    W1H = nc.dram_tensor("W1H", [32, 128, 8, 128], BF16, kind="Internal").ap()

    def bfv_col(buf, st, t, which):
        idx = ((st - 7) * 32 + t) * 2 + which
        return buf[:, idx:idx + 1]

    def vec_stages(buf, stages, swap_unused, first_wait):
        tmp1 = buf["tmp1"]; tmp2 = buf["tmp2"]; bfv = buf["bfv"]
        fw = list(first_wait or [])
        v_last = None
        for sti, st in enumerate(stages):
            hs = 1 << (st - 7)
            for ta in range(32):
                if ta & hs:
                    continue
                tb = ta + hs
                sc.op("vector", lambda q, st=st, ta=ta, tb=tb: q.tensor_scalar(
                    tmp1[:], GV[:, tb, :], bfv_col(bfv, st, ta, 1), None, ALU.mult),
                    waits=fw)
                fw = []
                sc.op("vector", lambda q, st=st, ta=ta, tb=tb: q.tensor_scalar(
                    tmp2[:], GV[:, ta, :], bfv_col(bfv, st, tb, 1), None, ALU.mult))
                sc.op("vector", lambda q, st=st, ta=ta, tb=tb: q.scalar_tensor_tensor(
                    GV[:, tb, :], GV[:, tb, :], bfv_col(bfv, st, tb, 0), tmp2[:],
                    ALU.mult, ALU.add))
                v_last = sc.op("vector", lambda q, st=st, ta=ta: q.scalar_tensor_tensor(
                    GV[:, ta, :], GV[:, ta, :], bfv_col(bfv, st, ta, 0), tmp1[:],
                    ALU.mult, ALU.add))
            if buf.get("with_bias"):
                bb = buf["bfvb"]
                bt = buf["bias_tmp"]
                nb_ = 32 // (2 * hs)
                gav = gb.rearrange("p (a two b) -> p a two b", two=2, b=hs)

                def bslice(k, sti=sti, bb=bb, nb_=nb_, hs=hs):
                    return bb[:, sti, k, :].rearrange("p (a b) -> p a b", b=hs)

                sc.op("vector", lambda q, gav=gav, sti=sti: q.tensor_tensor(
                    bt[:, 0:16].rearrange("p (a b) -> p a b", b=hs),
                    gav[:, :, 0, :], bslice(0), ALU.mult))
                sc.op("vector", lambda q, gav=gav, sti=sti: q.tensor_tensor(
                    bt[:, 16:32].rearrange("p (a b) -> p a b", b=hs),
                    gav[:, :, 1, :], bslice(1), ALU.mult))
                sc.op("vector", lambda q, gav=gav, sti=sti: q.tensor_tensor(
                    bt[:, 32:48].rearrange("p (a b) -> p a b", b=hs),
                    gav[:, :, 0, :], bslice(2), ALU.mult))
                sc.op("vector", lambda q, gav=gav, sti=sti: q.tensor_tensor(
                    bt[:, 48:64].rearrange("p (a b) -> p a b", b=hs),
                    gav[:, :, 1, :], bslice(3), ALU.mult))
                sc.op("vector", lambda q, gav=gav: q.tensor_tensor(
                    gav[:, :, 0, :],
                    bt[:, 0:16].rearrange("p (a b) -> p a b", b=hs),
                    bt[:, 16:32].rearrange("p (a b) -> p a b", b=hs), ALU.add))
                v_last = sc.op("vector", lambda q, gav=gav: q.tensor_tensor(
                    gav[:, :, 1, :],
                    bt[:, 32:48].rearrange("p (a b) -> p a b", b=hs),
                    bt[:, 48:64].rearrange("p (a b) -> p a b", b=hs), ALU.add))
        return v_last

    def merged_stages(init_ap, chain_aps, with_bias, first_wait, tag):
        with ExitStack() as tctx:
            TE = tctx.enter_context
            sbl = TE(nc.sbuf_tensor(f"sblm{tag}", [128, 2, 32, 128], BF16))
            Cb = TE(nc.sbuf_tensor(f"cbm{tag}", [128, 32, 128], BF16))
            CP = [TE(nc.psum_tensor(f"cpm{tag}{i}", [128, 128], F32)) for i in range(2)]
            GP = [TE(nc.psum_tensor(f"gpm{tag}{i}", [128, D], F32)) for i in range(2)]

            d_init = sc.dma_in("gpsimd", Cb[:], init_ap.rearrange("t j i -> j t i"))
            d_sb = {}
            d_sb[0] = sc.dma_in("gpsimd", sbl[:, 0, :, :], chain_aps[0].rearrange("t j i -> j t i"))
            if len(chain_aps) > 1:
                d_sb[1] = sc.dma_in("gpsimd", sbl[:, 1, :, :], chain_aps[1].rearrange("t j i -> j t i"))
            ev = {}
            for si in range(len(chain_aps)):
                par = si % 2
                for t in range(32):
                    m = sc.tmm(lambda q, si=si, t=t, par=par: q.matmul(
                        CP[t % 2][:], sbl[:, par, t, :], Cb[:, t, :],
                        start=True, stop=True),
                        waits=[d_init if (si == 0 and t == 0) else None,
                               d_sb[si] if t == 0 else None,
                               ev.get((si - 1, t)), ev.get((si, t - 2)),
                               ev.get((si - 1, 30 + t)) if t <= 1 else None],
                        mark=True)
                    eng = "scalar" if t % 2 == 0 else "vector"
                    if eng == "scalar":
                        ev[(si, t)] = sc.op("scalar", lambda q, t=t: q.copy(
                            Cb[:, t, :], CP[t % 2][:]), waits=[m])
                    else:
                        ev[(si, t)] = sc.op("vector", lambda q, t=t: q.tensor_copy(
                            Cb[:, t, :], CP[t % 2][:]), waits=[m])
                if si + 2 < len(chain_aps):
                    d_sb[si + 2] = sc.dma_in("gpsimd", sbl[:, par, :, :],
                                             chain_aps[si + 2].rearrange("t j i -> j t i"),
                                             waits=[m])
            last_si = len(chain_aps) - 1
            ga = {}
            mb_last = None
            for t in range(32):
                for (cs, wdt) in ((0, 512), (512, 512)):
                    ma = sc.tmm(lambda q, t=t, cs=cs: q.matmul(
                        GP[t % 2][:, cs:cs + 512], Cb[:, t, :], GV[:, t, cs:cs + 512],
                        start=True, stop=True),
                        waits=[first_wait if (t == 0 and cs == 0) else None,
                               ev.get((last_si, t)) if cs == 0 else None,
                               ga.get(t - 2) if cs == 0 else None],
                        mark=(cs == 512))
                if with_bias:
                    mb_last = sc.tmm(lambda q, t=t: q.matmul(
                        CP[0][:, t:t + 1], Cb[:, t, :], gb[:, t:t + 1],
                        start=True, stop=True),
                        waits=[ev.get((last_si, 30)) if t == 0 else None,
                               ev.get((last_si, 31)) if t == 0 else None],
                        mark=True)
                eng = "scalar" if t % 2 == 0 else "vector"
                wl = [ma if not with_bias else mb_last]
                if eng == "scalar":
                    ga[t] = sc.op("scalar", lambda q, t=t: q.copy(
                        GV[:, t, :], GP[t % 2][:]), waits=wl)
                else:
                    ga[t] = sc.op("vector", lambda q, t=t: q.tensor_copy(
                        GV[:, t, :], GP[t % 2][:]), waits=wl)
            if with_bias:
                bg = sc.op("vector", lambda q: q.tensor_copy(gb[:], CP[0][:, 0:32]),
                           waits=[mb_last])
                return ga[31], bg
            return ga[31], None

    with ExitStack() as pctx:
        PE = pctx.enter_context
        WEXc = PE(nc.sbuf_tensor("WEXc", [128, 2, 4096], BF16))
        PTW = [PE(nc.psum_tensor(f"PTWu{i}", [128, D], BF16)) for i in range(2)]

        sc.phase("ffn_b1")
        d_bex = sc.dma_in("sync", bexp_pm[:], io["b_exp_pm"][:])
        d_ub = sc.dma_in("sync", ubias_pm[:], io["up_bias_pm"][:])
        d_bu = sc.dma_in("sync", bfv_u[:], io["bfv_up"].rearrange("p a b c -> p (a b c)"))
        d_bd = sc.dma_in("sync", bfv_d[:], io["bfv_downT"].rearrange("p a b c -> p (a b c)"))
        d_bb = sc.dma_in("sync", bfvb_u[:], io["bfvb_up"][:])
        d_on = sc.dma_in("sync", ones_f[:], io["ones_f"][:])
        io["_tok"] = {"d_ub": d_ub, "d_bu": d_bu, "d_bd": d_bd, "d_on": d_on,
                      "d_bb": d_bb}
        v_gb = sc.op("vector", lambda q: q.tensor_copy(gb[:], bexp_pm[:]), waits=[d_bex])

        d_wc = {}
        d_wc[0] = sc.dma_in("gpsimd", WEXc[:, 0, :], io["W_exp"][0:128, :])
        d_wc[1] = sc.dma_in("gpsimd", WEXc[:, 1, :], io["W_exp"][128:256, :])
        ev = {}
        for c in range(8):
            par = c % 2
            for g in range(4):
                gpar = g % 2
                for tt in range(8):
                    t = g * 8 + tt
                    m = sc.tmm(lambda q, t=t, tt=tt, c=c, par=par, gpar=gpar: q.transpose(
                        PTW[gpar][:, tt * 128:(tt + 1) * 128],
                        WEXc[:, par, t * 128:(t + 1) * 128], id_s[:]),
                        waits=[d_wc[c] if (t == 0 and tt == 0) else None,
                               (ev.get((c - 1, 3)) if g == 0 else ev.get((c, g - 2)))
                               if tt == 0 else None] +
                              (io.get("tok_pre", []) if (c == 0 and t == 0) else []),
                        mark=(tt == 7))
                ev[(c, g)] = sc.op("vector", lambda q, g=g, c=c, gpar=gpar: q.tensor_copy(
                    GV[:, g * 8:(g + 1) * 8, c * 128:(c + 1) * 128],
                    PTW[gpar][:].rearrange("p (a b) -> p a b", a=8)), waits=[m])
            if c + 2 < 8:
                d_wc[c + 2] = sc.dma_in("gpsimd", WEXc[:, par, :],
                                        io["W_exp"][(c + 2) * 128:(c + 3) * 128, :],
                                        waits=[ev[(c, 3)]])
        ev_last = ev[(7, 3)]
        merged_stages(io["SUP"][6], [io["SUPN"][st] for st in range(5, -1, -1)],
                      True, ev_last, "u")
        sc.emit_ph()

    with ExitStack() as pctx:
        PE = pctx.enter_context
        tmp1 = PE(nc.sbuf_tensor("tmp1u", [128, D], BF16))
        tmp2 = PE(nc.sbuf_tensor("tmp2u", [128, D], BF16))
        WST = [PE(nc.sbuf_tensor(f"WSTu{i}", [128, 8, 128], BF16)) for i in range(2)]
        PTW = [PE(nc.psum_tensor(f"PTXu{i}", [128, D], BF16)) for i in range(2)]

        sc.phase("ffn_b1v")
        vec_stages({"tmp1": tmp1, "tmp2": tmp2, "bfv": bfv_u, "bfvb": bfvb_u,
                    "bias_tmp": bias_tmp, "with_bias": True}, range(7, 12), None,
                   [io["_tok"]["d_bu"], io["_tok"]["d_bb"]])
        v_gbias = sc.op("vector", lambda q: q.scalar_tensor_tensor(
            gbias[:], gb[:], 1.0, ubias_pm[:], ALU.mult, ALU.add),
            waits=[io["_tok"]["d_ub"]])

        dW = {}
        for t in range(32):
            par = t % 2
            for c in range(8):
                m = sc.tmm(lambda q, t=t, c=c, par=par: q.transpose(
                    PTW[par][:, c * 128:(c + 1) * 128],
                    GV[:, t, c * 128:(c + 1) * 128], id_s[:]),
                    waits=[(v_gbias if t == 0 else dW.get(t - 2)) if c == 0 else None],
                    mark=(c == 7))
            vv = sc.op("vector", lambda q, t=t, par=par: q.tensor_copy(
                WST[par][:].rearrange("p c f -> p (c f)"), PTW[par][:]),
                waits=[m, dW.get(t - 2)])
            dW[t] = sc.dma_out("sync", W1H[t], WST[par][:], waits=[vv])
        sc.drain_outs()
        sc.emit_ph()
    w1h_done = [dW[t] for t in range(32)]

    with ExitStack() as pctx:
        PE = pctx.enter_context
        tmp1 = PE(nc.sbuf_tensor("tmp1d", [128, D], BF16))
        tmp2 = PE(nc.sbuf_tensor("tmp2d", [128, D], BF16))
        brow = PE(nc.sbuf_tensor("brow", [1, D], F32))
        brow2 = PE(nc.sbuf_tensor("brow2", [1, D], F32))
        PBB = PE(nc.psum_tensor("PBBd", [128, D], F32))

        sc.phase("ffn_b2")
        d_wcon = sc.dma_in("gpsimd", GV[:], io["W_con"].rearrange("(t p) d -> p t d", p=128))
        d_bc1 = sc.dma_in("sync", brow[:], io["bcon_row"][:])
        d_bc2 = sc.dma_in("sync", brow2[:], io["dbias_row"][:])
        d_bc = sc.op("vector", lambda q: q.tensor_tensor(
            brow[:], brow[:], brow2[:], ALU.add), waits=[d_bc1, d_bc2])

        v_last = vec_stages({"tmp1": tmp1, "tmp2": tmp2, "bfv": bfv_d},
                            range(11, 6, -1), None, [d_wcon, io["_tok"]["d_bd"]])
        t_last, _ = merged_stages(io["SDOWN"][0], [io["SDOWNT"][st] for st in range(1, 7)],
                                  False, v_last, "d")

        for (cs, wdt) in ((0, 512), (512, 512)):
            sc.tmm(lambda q, cs=cs, wdt=wdt: q.matmul(
                PBB[:, cs:cs + wdt], ones_f[:], brow[:, cs:cs + wdt],
                start=True, stop=True),
                waits=[d_bc, t_last, io["_tok"]["d_on"]] if cs == 0 else [],
                mark=(cs == 512))
        sc.op("vector", lambda q: q.tensor_copy(bias_bc[:], PBB[:]),
              waits=[(sc.sem["tensor"], sc.sem["tensor"].val)])
        sc.emit_ph()

    with ExitStack() as pctx:
        PE = pctx.enter_context
        xTc = PE(nc.sbuf_tensor("xTc", [128, 8, 512], BF16))
        h3c = PE(nc.sbuf_tensor("h3c", [128, 32, 512], BF16))
        wr = PE(nc.sbuf_tensor("wr", [128, 3, 8, 128], BF16))
        SF2 = PE(nc.sbuf_tensor("SF2", [128, D], F32))
        SF3 = PE(nc.sbuf_tensor("SF3", [128, D], F32))
        bn6b = PE(nc.sbuf_tensor("bn6b", [128, 12], F32))
        mvb = PE(nc.sbuf_tensor("mvb", [128, 2], F32))
        vepsb = PE(nc.sbuf_tensor("vepsb", [128, 1], F32))
        sdb = PE(nc.sbuf_tensor("sdb", [128, 1], F32))
        rstdb = PE(nc.sbuf_tensor("rstdb", [128, 1], F32))
        OUTB = PE(nc.sbuf_tensor("OUTB", [128, 2, D], F32))
        PX = [PE(nc.psum_tensor(f"PXf{i}", [128, D], BF16)) for i in range(2)]
        PH = [PE(nc.psum_tensor(f"PHf{i}", [128, 512], F32)) for i in range(2)]
        PY = [PE(nc.psum_tensor(f"PYf{i}", [128, 512], F32)) for i in range(2)]

        NCHUNK = NSEQ * S // 512
        sc.phase("ffn_b3")
        sc.ph.add("sync", lambda q: q.nop(), w1h_done, ())
        evx = {}
        s_h3 = {}
        mh = {}
        d_w = {}
        my_marks = {}
        for gft in range(3):
            d_w[gft] = sc.dma_in("sync", wr[:, gft % 3, :, :], W1H[gft % 32])
        for cc in range(NCHUNK):
            gt0 = cc * 4
            for j in range(4):
                gtj = gt0 + j
                par = gtj % 2
                for c in range(8):
                    m = sc.tmm(lambda q, gtj=gtj, c=c, par=par: q.transpose(
                        PX[par][:, c * 128:(c + 1) * 128],
                        R[:, gtj, c * 128:(c + 1) * 128], id_s[:]),
                        waits=[evx.get(gtj - 2),
                               mh.get((cc - 1, 31)) if j == 0 else None]
                              if c == 0 else [],
                        mark=(c == 7))
                evx[gtj] = sc.op("vector", lambda q, j=j, par=par: q.tensor_copy(
                    xTc[:, :, j * 128:(j + 1) * 128], PX[par][:].rearrange(
                        "p (c f) -> p c f", c=8)), waits=[m])
            for ft in range(32):
                gft = cc * 32 + ft
                par = gft % 2
                for c in range(8):
                    mh[(cc, ft)] = sc.tmm(lambda q, cc=cc, ft=ft, c=c, par=par, gft=gft: q.matmul(
                        PH[par][:], wr[:, gft % 3, c, :], xTc[:, c, :],
                        start=(c == 0), stop=(c == 7)),
                        waits=([d_w[gft], evx[gt0 + 3], s_h3.get(gft - 2),
                                my_marks.get(cc - 1) if ft == 0 else None]
                               if c == 0 else []),
                        mark=(c == 7))
                s_h3[gft] = sc.op("scalar", lambda q, ft=ft, par=par: q.activation(
                    h3c[:, ft, :], PH[par][:], AF.Gelu, bias=gbias[:, ft:ft + 1]),
                    waits=[mh[(cc, ft)]])
                if gft + 3 < NCHUNK * 32:
                    d_w[gft + 3] = sc.dma_in("sync", wr[:, (gft + 3) % 3, :, :],
                                             W1H[(gft + 3) % 32], waits=[mh[(cc, ft)]])
            for tsub in range(4):
                gt = gt0 + tsub
                seq, i = gt // NT, gt % NT
                ppar = tsub % 2
                ev_y = None
                for dh in range(2):
                    for ft in range(32):
                        my = sc.tmm(lambda q, ft=ft, dh=dh, tsub=tsub, ppar=ppar: q.matmul(
                            PY[ppar][:], h3c[:, ft, tsub * 128:(tsub + 1) * 128],
                            GV[:, ft, dh * 512:(dh + 1) * 512],
                            start=(ft == 0), stop=(ft == 31)),
                            waits=([s_h3[cc * 32 + 31], io.get("ev_y_prev")]
                                   if (ft == 0 and dh == 0) else []),
                            mark=(ft == 31))
                    ev_y = sc.op("vector", lambda q, dh=dh, ppar=ppar: q.tensor_tensor(
                        SF2[:, dh * 512:(dh + 1) * 512], PY[ppar][:],
                        bias_bc[:, dh * 512:(dh + 1) * 512], ALU.add), waits=[my])
                    if dh == 0:
                        sc.tmm(lambda q: q.nop(), waits=[ev_y])
                io["ev_y_prev"] = ev_y
                my_marks[cc] = my
                sc.op("vector", lambda q, gt=gt: q.tensor_tensor(
                    SF3[:], SF2[:], R[:, gt, :], ALU.add))
                sc.op("vector", lambda q: q.bn_stats(bn6b[:, 0:6], SF3[:, 0:512]))
                sc.op("vector", lambda q: q.bn_stats(bn6b[:, 6:12], SF3[:, 512:1024]))
                v_eps = sc.op("vector", lambda q: q.bn_aggr(mvb[:], bn6b[:]))
                s_sd = sc.op("scalar", lambda q: q.activation(
                    sdb[:], mvb[:, 1:2], AF.Sqrt, bias=io["eps_t"][:]), waits=[v_eps])
                sc.op("vector", lambda q: q.reciprocal(rstdb[:], sdb[:]), waits=[s_sd])
                opar = gt % 2
                if inter:
                    sc.op("vector", lambda q, opar=opar: q.tensor_scalar(
                        OUTB[:, opar, :], SF3[:], mvb[:, 0:1], rstdb[:],
                        ALU.subtract, ALU.mult),
                        waits=[io["d_ffn_out"].get(gt - 2)])
                    v_r = sc.op("vector", lambda q, gt=gt, opar=opar: q.tensor_copy(
                        R[:, gt, :], OUTB[:, opar, :]))
                    io["d_ffn_out"][gt] = sc.dma_out(
                        "sync", h2[seq, i * 128:(i + 1) * 128, :], OUTB[:, opar, :],
                        waits=[v_r])
                else:
                    sc.op("vector", lambda q, gt=gt: q.tensor_scalar(
                        R[:, gt, :], SF3[:], mvb[:, 0:1], rstdb[:],
                        ALU.subtract, ALU.mult))
        sc.drain_outs()
        sc.emit_ph()
    bctx.close()


def host_l0_ffn_inputs(inputs):
    up_w = np.asarray(inputs["L0_ffn_up_weights"], np.float32)
    dn_w = np.asarray(inputs["L0_ffn_down_weights"], np.float32)
    b_exp = np.asarray(inputs["L0_ffn_b_exp"], np.float32)
    up_bias = np.asarray(inputs["L0_ffn_up_bias"], np.float32)
    b_con = np.asarray(inputs["L0_ffn_b_con"], np.float32)
    d_bias = np.asarray(inputs["L0_ffn_down_bias"], np.float32)

    def stage_dense(w_s, s, transpose_lhs):
        stride = 1 << s
        f = np.arange(4096)
        pair = (f >> (s + 1)) * stride + (f & (stride - 1))
        a = (f & stride) == 0
        diagv = np.where(a, w_s[pair, 0, 0], w_s[pair, 1, 1])
        offv = np.where(a, w_s[pair, 0, 1], w_s[pair, 1, 0])
        partner = f ^ stride
        out = np.zeros((32, 128, 128), np.float32)
        for t in range(32):
            sl = slice(128 * t, 128 * t + 128)
            i_loc = np.arange(128)
            j_off = partner[sl] - 128 * t
            if transpose_lhs:
                out[t][i_loc, i_loc] = diagv[sl]
                out[t][j_off, i_loc] = offv[sl]
            else:
                out[t][i_loc, i_loc] = diagv[sl]
                out[t][i_loc, j_off] = offv[sl]
        return out

    SUP = np.stack([stage_dense(up_w[s], s, True) for s in range(7)])
    SUPN = np.stack([stage_dense(up_w[s], s, False) for s in range(7)])
    SDOWN = np.stack([stage_dense(dn_w[s], s, False) for s in range(7)])
    SDOWNT = np.stack([stage_dense(dn_w[s], s, True) for s in range(7)])

    def bfv(w_all, swap):
        arr = np.zeros((128, 5, 32, 2), np.float32)
        for s in range(7, 12):
            stride = 1 << s
            f = np.arange(4096)
            pair = (f >> (s + 1)) * stride + (f & (stride - 1))
            a = (f & stride) == 0
            w_s = w_all[s]
            diagv = np.where(a, w_s[pair, 0, 0], w_s[pair, 1, 1])
            if not swap:
                offv = np.where(a, w_s[pair, 0, 1], w_s[pair, 1, 0])
            else:
                offv = np.where(a, w_s[pair, 1, 0], w_s[pair, 0, 1])
            for t in range(32):
                sl = slice(128 * t, 128 * t + 128)
                arr[:, s - 7, t, 0] = diagv[sl]
                arr[:, s - 7, t, 1] = offv[sl]
        return arr

    def bfvb(w_all):
        arr = np.zeros((128, 5, 4, 16), np.float32)
        for s_ in range(7, 12):
            stride = 1 << s_
            hs = 1 << (s_ - 7)
            f = np.arange(4096)
            pair = (f >> (s_ + 1)) * stride + (f & (stride - 1))
            a = (f & stride) == 0
            w_s = w_all[s_]
            diagv = np.where(a, w_s[pair, 0, 0], w_s[pair, 1, 1])
            offv = np.where(a, w_s[pair, 0, 1], w_s[pair, 1, 0])
            idx = 0
            for ta in range(32):
                if ta & hs:
                    continue
                tb = ta + hs
                sla = slice(128 * ta, 128 * ta + 128)
                slb = slice(128 * tb, 128 * tb + 128)
                arr[:, s_ - 7, 0, idx] = diagv[sla]
                arr[:, s_ - 7, 1, idx] = offv[sla]
                arr[:, s_ - 7, 2, idx] = offv[slb]
                arr[:, s_ - 7, 3, idx] = diagv[slb]
                idx += 1
        return arr

    def pm(v):
        return np.ascontiguousarray(v.reshape(-1, 128).T)

    return {
        "W_exp": np.ascontiguousarray(np.asarray(inputs["L0_ffn_W_exp"], np.float32)),
        "W_con": np.ascontiguousarray(np.asarray(inputs["L0_ffn_W_con"], np.float32)),
        "b_exp_pm": pm(b_exp),
        "up_bias_pm": pm(up_bias),
        "bfv_up": bfv(up_w, False),
        "bfvb_up": bfvb(up_w),
        "bfv_downT": bfv(dn_w, True),
        "SUP": SUP,
        "SUPN": SUPN,
        "SDOWN": SDOWN,
        "SDOWNT": SDOWNT,
        "bcon_row": np.ascontiguousarray(b_con.reshape(1, D)),
        "dbias_row": np.ascontiguousarray(d_bias.reshape(1, D)),
    }


def build_launch_b():
    nc = bass.Bass("TRN2", target_bir_lowering=False)
    io = {
        "h1": nc.dram_tensor("h1", [NSEQ, S, D], F32, kind="ExternalInput").ap(),
        "W_exp": nc.dram_tensor("W_exp", [D, 4096], F32, kind="ExternalInput").ap(),
        "W_con": nc.dram_tensor("W_con", [4096, D], F32, kind="ExternalInput").ap(),
        "b_exp_pm": nc.dram_tensor("b_exp_pm", [128, 32], F32, kind="ExternalInput").ap(),
        "up_bias_pm": nc.dram_tensor("up_bias_pm", [128, 32], F32, kind="ExternalInput").ap(),
        "bfv_up": nc.dram_tensor("bfv_up", [128, 5, 32, 2], F32, kind="ExternalInput").ap(),
        "bfvb_up": nc.dram_tensor("bfvb_up", [128, 5, 4, 16], F32, kind="ExternalInput").ap(),
        "bfv_downT": nc.dram_tensor("bfv_downT", [128, 5, 32, 2], F32, kind="ExternalInput").ap(),
        "SUP": nc.dram_tensor("SUP", [7, 32, 128, 128], F32, kind="ExternalInput").ap(),
        "SUPN": nc.dram_tensor("SUPN", [7, 32, 128, 128], F32, kind="ExternalInput").ap(),
        "SDOWN": nc.dram_tensor("SDOWN", [7, 32, 128, 128], F32, kind="ExternalInput").ap(),
        "SDOWNT": nc.dram_tensor("SDOWNT", [7, 32, 128, 128], F32, kind="ExternalInput").ap(),
        "bcon_row": nc.dram_tensor("bcon_row", [1, D], F32, kind="ExternalInput").ap(),
        "dbias_row": nc.dram_tensor("dbias_row", [1, D], F32, kind="ExternalInput").ap(),
        "ones_f": nc.dram_tensor("ones_f", [1, 128], F32, kind="ExternalInput").ap(),
        "ident": nc.dram_tensor("ident", [128, 128], BF16, kind="ExternalInput").ap(),
        "h2": nc.dram_tensor("h2", [NSEQ, S, D], F32, kind="ExternalOutput").ap(),
    }
    with ExitStack() as ctx:
        E = ctx.enter_context
        io["R"] = E(nc.sbuf_tensor("R", [128, NSEQ * NT, D], BF16))
        io["id_s"] = E(nc.sbuf_tensor("id_s", [128, 128], BF16))
        io["ones_fs"] = E(nc.sbuf_tensor("ones_fs", [1, 128], F32))
        io["d_ffn_out"] = {}
        io["eps_t"] = E(nc.sbuf_tensor("eps_t", [128, 1], F32))
        sc = Sched(nc, ctx, "b")
        sc.phase("pre")
        sc.op("vector", lambda q: q.memset(io["eps_t"][:], EPS))
        t1 = sc.dma_in("sync", io["id_s"][:], io["ident"][:])
        t2 = sc.dma_in("sync", io["ones_fs"][:], io["ones_f"][:])
        io["tok_pre"] = [t1, t2]
        sc.emit_ph()
        emit_load_R(sc, io, io["h1"])
        emit_l0_ffn(sc, E, io)
        sc.finalize()
    return nc





def emit_l1_attn(sc: Sched, E, io):
    nc = sc.nc
    R, id_s = io["R"], io["id_s"]
    h3 = io.get("h3")
    inter = io.get("write_inter", True)
    ones_f = io["ones_fs"]

    with ExitStack() as sctx:
        SE = sctx.enter_context
        wA = SE(nc.sbuf_tensor("l1wA", [128, 8, D], BF16))
        wB = SE(nc.sbuf_tensor("l1wB", [128, 8, D], BF16))
        onec = SE(nc.sbuf_tensor("onec", [128, 1], BF16))
        borow = SE(nc.sbuf_tensor("borow", [1, D], F32))
        bo_bc = SE(nc.sbuf_tensor("bo_bc", [128, D], F32))
        ctx_s = SE(nc.sbuf_tensor("ctx_s", [128, 8, D], BF16))
        xTr = SE(nc.sbuf_tensor("xTr", [128, 2, 8, 128], BF16))
        qsum = SE(nc.sbuf_tensor("qsum", [128, 1], F32))
        qrec = SE(nc.sbuf_tensor("qrec", [128, 1], F32))
        zcol = SE(nc.sbuf_tensor("zcol", [128, 8], F32))
        zzs = SE(nc.sbuf_tensor("zzs", [128, 8], F32))
        SF2 = SE(nc.sbuf_tensor("l1SF2", [128, D], F32))
        SF3 = SE(nc.sbuf_tensor("l1SF3", [128, D], F32))
        bn6c = SE(nc.sbuf_tensor("bn6c", [128, 12], F32))
        mvc = SE(nc.sbuf_tensor("mvc", [128, 2], F32))
        vepsc = SE(nc.sbuf_tensor("vepsc", [128, 1], F32))
        sdc = SE(nc.sbuf_tensor("sdc", [128, 1], F32))
        rstdc = SE(nc.sbuf_tensor("rstdc", [128, 1], F32))
        mnrc = SE(nc.sbuf_tensor("mnrc", [128, 1], F32))
        OUTC = SE(nc.sbuf_tensor("OUTC", [128, 1, D], F32))

        d_wk = d_wv = None
        for seq in range(NSEQ):
            Roff = seq * NT
            with ExitStack() as ektx:
                ek = ektx.enter_context(nc.sbuf_tensor(f"ek{seq}", [128, NT, D], BF16))
                vt = ektx.enter_context(nc.sbuf_tensor(f"vt{seq}", [128, NT, D], BF16))

                with ExitStack() as pctx:
                    PE = pctx.enter_context
                    XTP = [PE(nc.psum_tensor(f"XTP{seq}_{i}", [128, D], BF16)) for i in range(2)]
                    KPP = [PE(nc.psum_tensor(f"KPP{seq}_{i}", [128, D], F32)) for i in range(2)]

                    sc.phase(f"l1s{seq}_d1")
                    if seq == 0:
                        d_wk = sc.dma_in("gpsimd", wA[:], io["Wk"].rearrange("(c p) o -> p c o", p=128))
                        d_wv = sc.dma_in("gpsimd", wB[:], io["Wv"].rearrange("(c p) o -> p c o", p=128))
                        d_oc = sc.dma_in("sync", onec[:], io["ones_col"][:])
                        d_bo = sc.dma_in("sync", borow[:], io["bo_row"][:])
                        io["_l1tok"] = {"d_oc": d_oc, "d_bo": d_bo}
                    ev = {}
                    s_k = {}
                    for t in range(NT):
                        par = t % 2
                        for c in range(8):
                            m = sc.tmm(lambda q, t=t, c=c, par=par: q.transpose(
                                XTP[par][:, c * 128:(c + 1) * 128],
                                R[:, Roff + t, c * 128:(c + 1) * 128], id_s[:]),
                                waits=[ev.get(t - 2) if c == 0 else None] +
                                      (io.get("tok_pre", [])
                                       if (seq == 0 and t == 0 and c == 0) else []),
                                mark=(c == 7))
                        ev[t] = sc.op("vector", lambda q, par=par: q.tensor_copy(
                            xTr[:, par, :, :].rearrange("p a b -> p (a b)"), XTP[par][:]),
                            waits=[m])
                        for (cs, wdt) in ((0, 512), (512, 512)):
                            for c in range(8):
                                mk = sc.tmm(lambda q, c=c, cs=cs, par=par: q.matmul(
                                    KPP[par][:, cs:cs + 512], xTr[:, par, c, :],
                                    wA[:, c, cs:cs + 512], start=(c == 0), stop=(c == 7)),
                                    waits=[ev[t], d_wk, s_k.get(t - 2)] if (c == 0 and cs == 0) else [],
                                    mark=(c == 7 and cs == 512))
                        s_k[t] = sc.op("scalar", lambda q, t=t, par=par: q.activation(
                            ek[:, t, :], KPP[par][:], AF.Exp), waits=[mk])
                    sc.emit_ph()

                with ExitStack() as pctx:
                    PE = pctx.enter_context
                    XTP = [PE(nc.psum_tensor(f"XTQ{seq}_{i}", [128, D], BF16)) for i in range(2)]
                    VPP = [PE(nc.psum_tensor(f"VPP{seq}_{i}", [128, D], F32)) for i in range(2)]

                    sc.phase(f"l1s{seq}_d2")
                    ev = {}
                    s_v = {}
                    for t in range(NT):
                        par = t % 2
                        for c in range(8):
                            m = sc.tmm(lambda q, t=t, c=c, par=par: q.transpose(
                                XTP[par][:, c * 128:(c + 1) * 128],
                                R[:, Roff + t, c * 128:(c + 1) * 128], id_s[:]),
                                waits=[ev.get(t - 2) if c == 0 else None],
                                mark=(c == 7))
                        ev[t] = sc.op("vector", lambda q, par=par: q.tensor_copy(
                            xTr[:, par, :, :].rearrange("p a b -> p (a b)"), XTP[par][:]),
                            waits=[m])
                        for (cs, wdt) in ((0, 512), (512, 512)):
                            for c in range(8):
                                mv_ = sc.tmm(lambda q, c=c, cs=cs, par=par: q.matmul(
                                    VPP[par][:, cs:cs + 512], xTr[:, par, c, :],
                                    wB[:, c, cs:cs + 512], start=(c == 0), stop=(c == 7)),
                                    waits=[ev[t], d_wv, s_v.get(t - 2)] if (c == 0 and cs == 0) else [],
                                    mark=(c == 7 and cs == 512))
                        s_v[t] = sc.op("scalar", lambda q, t=t, par=par: q.copy(
                            vt[:, t, :], VPP[par][:]), waits=[mv_])
                    d_wq = sc.dma_in("gpsimd", wA[:], io["Wq"].rearrange("(c p) o -> p c o", p=128),
                                     waits=[mk])
                    sc.emit_ph()

                with ExitStack() as pctx:
                    PE = pctx.enter_context
                    ZP = PE(nc.psum_tensor(f"ZPx{seq}", [1, D], F32))
                    ZTC = PE(nc.psum_tensor(f"ZTCx{seq}", [128, 8], F32))
                    CXP = [PE(nc.psum_tensor(f"CXP{seq}_{i}", [128, 512], F32)) for i in range(2)]

                    sc.phase(f"l1s{seq}_d3")
                    for (cs, wdt) in ((0, 512), (512, 512)):
                        for t in range(NT):
                            mz = sc.tmm(lambda q, t=t, cs=cs: q.matmul(
                                ZP[:, cs:cs + 512], onec[:], ek[:, t, cs:cs + 512],
                                start=(t == 0), stop=(t == NT - 1)),
                                waits=[io["_l1tok"]["d_oc"]] if (t == 0 and cs == 0) else [],
                                mark=(t == NT - 1 and cs == 512))
                    s_zr = sc.op("scalar", lambda q: q.copy(SF2[0:1, :], ZP[:]), waits=[mz])
                    mzt = None
                    for dt in range(8):
                        mzt = sc.tmm(lambda q, dt=dt: q.transpose(
                            ZTC[:, dt:dt + 1], SF2[0:1, dt * 128:(dt + 1) * 128],
                            io["id_f"][0:1, 0:1]), waits=[s_zr if dt == 0 else None],
                            mark=(dt == 7))
                    sc.op("vector", lambda q: q.tensor_copy(zcol[:], ZTC[:]), waits=[mzt])
                    sc.op("vector", lambda q: q.reciprocal(zzs[:], zcol[:]))
                    v_z3 = sc.op("vector", lambda q: q.tensor_scalar(
                        zzs[:], zzs[:], 0.125 / 1024.0, None, ALU.mult))
                    ctx_ev = {}
                    for dt in range(8):
                        for ecs in range(2):
                            k2 = dt * 2 + ecs
                            par = k2 % 2
                            for t in range(NT):
                                mc = sc.tmm(lambda q, dt=dt, ecs=ecs, t=t, par=par: q.matmul(
                                    CXP[par][:], ek[:, t, dt * 128:(dt + 1) * 128],
                                    vt[:, t, ecs * 512:(ecs + 1) * 512],
                                    start=(t == 0), stop=(t == NT - 1)),
                                    waits=[ctx_ev.get(k2 - 2)] if t == 0 else [],
                                    mark=(t == NT - 1))
                            ctx_ev[k2] = sc.op(
                                "vector", lambda q, dt=dt, ecs=ecs, par=par: q.tensor_scalar(
                                    ctx_s[:, dt, ecs * 512:(ecs + 1) * 512], CXP[par][:],
                                    zzs[:, dt:dt + 1], None, ALU.mult),
                                waits=[mc, v_z3 if k2 == 0 else None])
                    d_wo = sc.dma_in("gpsimd", wB[:],
                                     io["Wo"].rearrange("(c p) o -> p c o", p=128),
                                     waits=[mc])
                    sc.emit_ph()

            with ExitStack() as d4ctx:
                qTf = d4ctx.enter_context(nc.sbuf_tensor(f"qTf{seq}", [128, 8, S], BF16))
                pctx = d4ctx.enter_context(ExitStack())
                PE = pctx.enter_context
                qht = PE(nc.sbuf_tensor(f"qht{seq}", [128, 2, D], BF16))
                TPP = [PE(nc.psum_tensor(f"TPa{seq}_{i}", [128, D], BF16)) for i in range(3)]
                QPP = [PE(nc.psum_tensor(f"QPx{seq}_{i}", [128, D], F32)) for i in range(2)]

                sc.phase(f"l1s{seq}_d4a")
                ev_tp = {}
                s_q = {}
                v_rc = {}
                use = [0]

                def tp_group(src_fn, dst_fn, wait0):
                    u = use[0]
                    use[0] += 1
                    slot = u % 3
                    for c in range(8):
                        m = sc.tmm(lambda q, c=c, slot=slot, src_fn=src_fn: q.transpose(
                            TPP[slot][:, c * 128:(c + 1) * 128],
                            src_fn(c), id_s[:]),
                            waits=[wait0, ev_tp.get(u - 3)] if c == 0 else [],
                            mark=(c == 7))
                    ev_tp[u] = sc.op("vector", lambda q, slot=slot, dst_fn=dst_fn: q.tensor_copy(
                        dst_fn(), TPP[slot][:]), waits=[m])
                    return ev_tp[u]

                def emit_qht_tp(t):
                    par = t % 2
                    return tp_group(
                        lambda c, t=t, par=par: qht[:, par, c * 128:(c + 1) * 128],
                        lambda t=t: qTf[:, :, t * 128:(t + 1) * 128],
                        s_q[t])

                for t in range(NT):
                    par = t % 2
                    exr = tp_group(
                        lambda c, t=t: R[:, Roff + t, c * 128:(c + 1) * 128],
                        lambda par=par: xTr[:, par, :, :].rearrange("p a b -> p (a b)"),
                        None)
                    for (cs, wdt) in ((0, 512), (512, 512)):
                        for c in range(8):
                            mq = sc.tmm(lambda q, c=c, cs=cs, par=par: q.matmul(
                                QPP[par][:, cs:cs + 512], xTr[:, par, c, :],
                                wA[:, c, cs:cs + 512], start=(c == 0), stop=(c == 7)),
                                waits=[exr, d_wq, s_q.get(t - 2)]
                                      if (c == 0 and cs == 0) else [],
                                mark=(c == 7 and cs == 512))
                    sq = sc.op("scalar", lambda q, par=par: q.activation(
                        qht[:, par, :], QPP[par][:], AF.Exp, accum_out=qsum[:]),
                        waits=[mq, v_rc.get(t - 1)])
                    v_rc[t] = sc.op("vector", lambda q: q.reciprocal(qrec[:], qsum[:]),
                                    waits=[sq])
                    s_q[t] = sc.op("vector", lambda q, par=par: q.tensor_scalar(
                        qht[:, par, :], qht[:, par, :], qrec[:], None, ALU.mult))
                    if t >= 1:
                        emit_qht_tp(t - 1)
                emit_qht_tp(NT - 1)
                sc.emit_ph()
                pctx.close()

                with ExitStack() as qctx:
                    QE = qctx.enter_context
                    ytk = QE(nc.sbuf_tensor(f"ytk{seq}", [128, 2, D], BF16))
                    yTr = QE(nc.sbuf_tensor(f"yTr{seq}", [128, 2, 8, 128], BF16))
                    TPB = [QE(nc.psum_tensor(f"TPb{seq}_{i}", [128, D], BF16)) for i in range(2)]
                    YP = [QE(nc.psum_tensor(f"YPx{seq}_{i}", [128, D], F32)) for i in range(2)]
                    OP = QE(nc.psum_tensor(f"OPx{seq}", [128, D], F32))

                    sc.phase(f"l1s{seq}_d4b")
                    if seq == 0:
                        for (cs, wdt) in ((0, 512), (512, 512)):
                            mb = sc.tmm(lambda q, cs=cs: q.matmul(
                                OP[:, cs:cs + 512], ones_f[:], borow[:, cs:cs + 512],
                                start=True, stop=True),
                                waits=[io["_l1tok"]["d_bo"]] if cs == 0 else [],
                                mark=(cs == 512))
                        v_bo = sc.op("vector", lambda q: q.tensor_copy(bo_bc[:], OP[:]),
                                     waits=[mb])
                    yguard = {}
                    s_o = {}
                    v_y = {}

                    def emit_tail(t):
                        gt = Roff + t
                        par = t % 2
                        for c in range(8):
                            m3 = sc.tmm(lambda q, c=c, par=par: q.transpose(
                                TPB[par][:, c * 128:(c + 1) * 128],
                                ytk[:, par, c * 128:(c + 1) * 128], id_s[:]),
                                waits=[v_y[t], yguard.get(t - 2),
                                       v_bo if (seq == 0 and t == 0) else None]
                                      if c == 0 else [],
                                mark=(c == 7))
                        yguard[t] = sc.op("vector", lambda q, par=par: q.tensor_copy(
                            yTr[:, par, :, :].rearrange("p a b -> p (a b)"), TPB[par][:]),
                            waits=[m3])
                        for (cs, wdt) in ((0, 512), (512, 512)):
                            for c in range(8):
                                mo = sc.tmm(lambda q, c=c, cs=cs, par=par: q.matmul(
                                    OP[:, cs:cs + 512], yTr[:, par, c, :],
                                    wB[:, c, cs:cs + 512], start=(c == 0), stop=(c == 7)),
                                    waits=[yguard[t], d_wo] if (c == 0 and cs == 0) else [],
                                    mark=(c == 7 and cs == 512))
                        return gt, mo

                    def emit_ln(gt, mo, t):
                        v_sf2 = sc.op("vector", lambda q: q.tensor_tensor(
                            SF2[:], OP[:], bo_bc[:], ALU.add), waits=[mo])
                        g_sf3 = sc.op("gpsimd", lambda q, gt=gt: q.tensor_tensor(
                            SF3[:], SF2[:], R[:, gt, :], ALU.add), waits=[v_sf2])
                        sc.op("vector", lambda q: q.bn_stats(bn6c[:, 0:6], SF3[:, 0:512]),
                              waits=[g_sf3])
                        sc.op("vector", lambda q: q.bn_stats(bn6c[:, 6:12], SF3[:, 512:1024]))
                        v_eps = sc.op("vector", lambda q: q.bn_aggr(mvc[:], bn6c[:]))
                        s_ln = sc.op("scalar", lambda q: q.activation(
                            sdc[:], mvc[:, 1:2], AF.Ln, bias=io["eps_t"][:]), waits=[v_eps])
                        s_sd = sc.op("scalar", lambda q: q.activation(
                            rstdc[:], sdc[:], AF.Exp, scale=-0.5))
                        v_mnr = sc.op("vector", lambda q: q.tensor_scalar(
                            mnrc[:], mvc[:, 0:1], rstdc[:], -1.0, ALU.mult, ALU.mult),
                            waits=[s_sd])
                        if inter:
                            sc.op("vector", lambda q: q.tensor_scalar(
                                OUTC[:, 0, :], SF3[:], mvc[:, 0:1], rstdc[:],
                                ALU.subtract, ALU.mult),
                                waits=[s_sd, io["d_l1_out"].get(gt - 1)])
                            s_o[t] = sc.op("vector", lambda q, gt=gt: q.tensor_copy(
                                R[:, gt, :], OUTC[:, 0, :]))
                            io["d_l1_out"][gt] = sc.dma_out(
                                "sync", h3[seq, t * 128:(t + 1) * 128, :], OUTC[:, 0, :],
                                waits=[s_o[t]])
                        else:
                            s_o[t] = sc.op("gpsimd", lambda q, gt=gt: q.tensor_scalar(
                                R[:, gt, :], SF3[:], rstdc[:], mnrc[:],
                                ALU.mult, ALU.add),
                                waits=[v_mnr])

                    pending = None
                    for t in range(NT):
                        par = t % 2
                        for (cs, wdt) in ((0, 512), (512, 512)):
                            for c in range(8):
                                my = sc.tmm(lambda q, t=t, c=c, cs=cs, par=par: q.matmul(
                                    YP[par][:, cs:cs + 512], qTf[:, c, t * 128:(t + 1) * 128],
                                    ctx_s[:, c, cs:cs + 512], start=(c == 0), stop=(c == 7)),
                                    waits=[v_y.get(t - 2)] if (c == 0 and cs == 0) else [],
                                    mark=(c == 7 and cs == 512))
                        v_y[t] = sc.op("vector", lambda q, par=par: q.tensor_copy(
                            ytk[:, par, :], YP[par][:]), waits=[my])
                        if pending is not None:
                            gtp, mop = emit_tail(pending)
                            emit_ln(gtp, mop, pending)
                        pending = t
                    gtp, mop = emit_tail(pending)
                    emit_ln(gtp, mop, pending)
                if seq == 0:
                    d_wk = sc.dma_in("gpsimd", wA[:],
                                     io["Wk"].rearrange("(c p) o -> p c o", p=128),
                                     waits=[mop])
                    d_wv = sc.dma_in("gpsimd", wB[:],
                                     io["Wv"].rearrange("(c p) o -> p c o", p=128),
                                     waits=[mop])
                sc.drain_outs()
                sc.emit_ph()


def build_launch_c():
    nc = bass.Bass("TRN2", target_bir_lowering=False)
    io = {
        "h2in": nc.dram_tensor("h2in", [NSEQ, S, D], F32, kind="ExternalInput").ap(),
        "Wq": nc.dram_tensor("Wq", [D, D], F32, kind="ExternalInput").ap(),
        "Wk": nc.dram_tensor("Wk", [D, D], F32, kind="ExternalInput").ap(),
        "Wv": nc.dram_tensor("Wv", [D, D], F32, kind="ExternalInput").ap(),
        "Wo": nc.dram_tensor("Wo", [D, D], F32, kind="ExternalInput").ap(),
        "bo_row": nc.dram_tensor("bo_row", [1, D], F32, kind="ExternalInput").ap(),
        "ones_col": nc.dram_tensor("ones_col", [128, 1], BF16, kind="ExternalInput").ap(),
        "ones_f": nc.dram_tensor("ones_f", [1, 128], F32, kind="ExternalInput").ap(),
        "ident": nc.dram_tensor("ident", [128, 128], BF16, kind="ExternalInput").ap(),
        "identf": nc.dram_tensor("identf", [128, 128], F32, kind="ExternalInput").ap(),
        "h3": nc.dram_tensor("h3", [NSEQ, S, D], F32, kind="ExternalOutput").ap(),
    }
    with ExitStack() as ctx:
        E = ctx.enter_context
        io["R"] = E(nc.sbuf_tensor("R", [128, NSEQ * NT, D], BF16))
        io["id_s"] = E(nc.sbuf_tensor("id_s", [128, 128], BF16))
        io["id_f"] = E(nc.sbuf_tensor("id_f", [128, 128], F32))
        io["ones_fs"] = E(nc.sbuf_tensor("ones_fs", [1, 128], F32))
        io["d_l1_out"] = {}
        io["eps_t"] = E(nc.sbuf_tensor("eps_t", [128, 1], F32))
        sc = Sched(nc, ctx, "c")
        sc.phase("pre")
        sc.op("vector", lambda q: q.memset(io["eps_t"][:], EPS))
        t1 = sc.dma_in("sync", io["id_s"][:], io["ident"][:])
        t2 = sc.dma_in("sync", io["id_f"][:], io["identf"][:])
        t3 = sc.dma_in("sync", io["ones_fs"][:], io["ones_f"][:])
        io["tok_pre"] = [t1, t2, t3]
        sc.emit_ph()
        emit_load_R(sc, io, io["h2in"])
        emit_l1_attn(sc, E, io)
        sc.finalize()
    return nc


def host_l1_attn_inputs(inputs):
    return {
        "Wq": np.ascontiguousarray(np.asarray(inputs["L1_attn_Wq"], np.float32)),
        "Wk": np.ascontiguousarray(np.asarray(inputs["L1_attn_Wk"], np.float32)),
        "Wv": np.ascontiguousarray(np.asarray(inputs["L1_attn_Wv"], np.float32)),
        "Wo": np.ascontiguousarray(np.asarray(inputs["L1_attn_Wo"], np.float32)),
        "bo_row": np.ascontiguousarray(np.asarray(inputs["L1_attn_bo"], np.float32).reshape(1, D)),
        "ones_col": np.ones((128, 1), ml_dtypes.bfloat16),
        "ones_f": np.ones((1, 128), np.float32),
        "ident": np.eye(128, dtype=ml_dtypes.bfloat16),
        "identf": np.eye(128, dtype=np.float32),
    }




def emit_l1_moe(sc: Sched, E, io):
    nc = sc.nc
    R, id_s = io["R"], io["id_s"]
    out = io["out"]
    NTT = NSEQ * NT

    bctx = ExitStack()
    E = bctx.enter_context
    comb = E(nc.sbuf_tensor("comb", [128, NTT, 8], F32))
    routT = E(nc.sbuf_tensor("routT", [128, 8, 8], BF16))

    GUT = nc.dram_tensor("GUT", [8, 16, 128, 8, 128], BF16, kind="Internal").ap()
    DPT = nc.dram_tensor("DPT", [8, 128, 8, D], BF16, kind="Internal").ap()

    with ExitStack() as pctx:
        PE = pctx.enter_context
        gub = PE(nc.sbuf_tensor("gub", [128, 2, 16, D], BF16))
        dpb = PE(nc.sbuf_tensor("dpb", [128, 2, 8, D], BF16))
        WSTD = [PE(nc.sbuf_tensor(f"WSTD{i}", [128, 8, 128], BF16)) for i in range(2)]
        PTD = [PE(nc.psum_tensor(f"PTD{i}", [128, D], BF16)) for i in range(2)]

        d_gu = {}
        d_dp = {}
        sc.phase("moe_m0")
        d_gu[0] = sc.dma_in("gpsimd", gub[:, 0, :, :],
                            io["gate_up"][0].rearrange("(t p) h -> p t h", p=128))
        d_dp[0] = sc.dma_in("gpsimd", dpb[:, 0, :, :],
                            io["down_p"][0].rearrange("(c p) f -> p c f", p=128))
        dW = {}
        m_last = {}
        for e in range(8):
            sl = e % 2
            if e + 1 < 8:
                d_gu[e + 1] = sc.dma_in("gpsimd", gub[:, 1 - sl, :, :],
                                        io["gate_up"][e + 1].rearrange("(t p) h -> p t h", p=128),
                                        waits=[m_last.get(e - 1)])
                d_dp[e + 1] = sc.dma_in("gpsimd", dpb[:, 1 - sl, :, :],
                                        io["down_p"][e + 1].rearrange("(c p) f -> p c f", p=128),
                                        waits=[None])
            for ft in range(16):
                par = ft % 2
                for c in range(8):
                    m = sc.tmm(lambda q, ft=ft, c=c, par=par, sl=sl: q.transpose(
                        PTD[par][:, c * 128:(c + 1) * 128],
                        gub[:, sl, ft, c * 128:(c + 1) * 128], id_s[:]),
                        waits=[d_gu[e] if (ft == 0 and c == 0) else None,
                               dW.get((e, ft - 2)) if c == 0 else None] +
                              (io.get("tok_pre", [])
                               if (e == 0 and ft == 0 and c == 0) else []),
                        mark=(c == 7))
                vv = sc.op("vector", lambda q, par=par: q.tensor_copy(
                    WSTD[par][:].rearrange("p c f -> p (c f)"), PTD[par][:]),
                    waits=[m, dW.get((e, ft - 2))])
                dW[(e, ft)] = sc.dma_out("sync", GUT[e, ft], WSTD[par][:], waits=[vv])
            for fc in range(8):
                par = fc % 2
                for hc in range(8):
                    m = sc.tmm(lambda q, fc=fc, hc=hc, par=par, sl=sl: q.transpose(
                        PTD[par][:, hc * 128:(hc + 1) * 128],
                        dpb[:, sl, hc, fc * 128:(fc + 1) * 128], id_s[:]),
                        waits=[d_dp[e] if (fc == 0 and hc == 0) else None,
                               dW.get((e, 16 + fc - 2)) if hc == 0 else None],
                        mark=(hc == 7))
                vv = sc.op("vector", lambda q, par=par: q.tensor_copy(
                    WSTD[par][:].rearrange("p c f -> p (c f)"), PTD[par][:]),
                    waits=[m, dW.get((e, 16 + fc - 2))])
                dW[(e, 16 + fc)] = sc.dma_out("sync", DPT[e][:, fc, :], WSTD[par][:],
                                              waits=[vv])
            m_last[e] = m
            dW[(e + 1, -2)] = dW[(e, 22)]
            dW[(e + 1, -1)] = dW[(e, 23)]
        sc.drain_outs()
        sc.emit_ph()

    with ExitStack() as pctx:
        PE = pctx.enter_context
        rw_s = PE(nc.sbuf_tensor("rw_s", [8, D], BF16))
        xTr = PE(nc.sbuf_tensor("xTrm", [128, 2, 8, 128], BF16))
        m1 = PE(nc.sbuf_tensor("m1", [128, 1], F32))
        m2 = PE(nc.sbuf_tensor("m2", [128, 1], F32))
        d12 = PE(nc.sbuf_tensor("d12", [128, 1], F32))
        w1t = PE(nc.sbuf_tensor("w1t", [128, 1], F32))
        w2t = PE(nc.sbuf_tensor("w2t", [128, 1], F32))
        eq1 = PE(nc.sbuf_tensor("eq1", [128, 8], F32))
        eq2 = PE(nc.sbuf_tensor("eq2", [128, 8], F32))
        v2b = PE(nc.sbuf_tensor("v2b", [128, 8], F32))
        tmp8 = PE(nc.sbuf_tensor("tmp8", [128, 8], F32))
        PXT = [PE(nc.psum_tensor(f"PXTm{i}", [128, D], BF16)) for i in range(2)]
        RTP = PE(nc.psum_tensor("RTP", [128, 8], BF16))
        RLP = [PE(nc.psum_tensor(f"RLP{i}", [128, 8], F32)) for i in range(2)]

        sc.phase("moe_m1")
        d_rw = sc.dma_in("gpsimd", rw_s[:], io["router_w"][:])
        mrt = None
        s_rt = None
        for c in range(8):
            mrt = sc.tmm(lambda q, c=c: q.transpose(
                RTP[:, 0:8], rw_s[:, c * 128:(c + 1) * 128], id_s[0:8, 0:8]),
                waits=[d_rw if c == 0 else None,
                       (sc.sem["scalar"], sc.sem["scalar"].val) if c > 0 else None],
                mark=True)
            s_rt = sc.op("scalar", lambda q, c=c: q.copy(routT[:, c, :], RTP[:, 0:8]),
                         waits=[mrt])
        ev = {}
        s_cb = {}
        for gt in range(NTT):
            par = gt % 2
            for c in range(8):
                m = sc.tmm(lambda q, gt=gt, c=c, par=par: q.transpose(
                    PXT[par][:, c * 128:(c + 1) * 128],
                    R[:, gt, c * 128:(c + 1) * 128], id_s[:]),
                    waits=[ev.get(gt - 2) if c == 0 else None],
                    mark=(c == 7))
            ev[gt] = sc.op("vector", lambda q, par=par: q.tensor_copy(
                xTr[:, par, :, :].rearrange("p a b -> p (a b)"), PXT[par][:]),
                waits=[m])
            for c in range(8):
                ml = sc.tmm(lambda q, gt=gt, c=c, par=par: q.matmul(
                    RLP[par][:], xTr[:, par, c, :], routT[:, c, :],
                    start=(c == 0), stop=(c == 7)),
                    waits=[ev[gt], s_cb.get(gt - 2),
                           s_rt if gt == 0 else None] if c == 0 else [],
                    mark=(c == 7))
            sc.op("vector", lambda q, par=par: q.reduce_max(m1[:], RLP[par][:], axis=AX.X),
                  waits=[ml])
            sc.op("vector", lambda q, par=par: q.tensor_scalar(
                eq1[:], RLP[par][:], m1[:], None, ALU.is_equal))
            sc.op("vector", lambda q, par=par: q.scalar_tensor_tensor(
                v2b[:], eq1[:], -1e30, RLP[par][:], ALU.mult, ALU.add))
            sc.op("vector", lambda q: q.reduce_max(m2[:], v2b[:], axis=AX.X))
            v_d = sc.op("vector", lambda q: q.tensor_tensor(d12[:], m1[:], m2[:], ALU.subtract))
            s_w1 = sc.op("scalar", lambda q: q.activation(w1t[:], d12[:], AF.Sigmoid),
                         waits=[v_d])
            sc.op("vector", lambda q: q.tensor_scalar(
                w2t[:], w1t[:], -1.0, 1.0, ALU.mult, ALU.add), waits=[s_w1])
            sc.op("vector", lambda q, par=par: q.tensor_scalar(
                eq2[:], RLP[par][:], m2[:], None, ALU.is_equal))
            sc.op("vector", lambda q: q.tensor_scalar(
                tmp8[:], eq1[:], w1t[:], None, ALU.mult))
            s_cb[gt] = sc.op("vector", lambda q, gt=gt: q.scalar_tensor_tensor(
                comb[:, gt, :], eq2[:], w2t[:], tmp8[:], ALU.mult, ALU.add))
        sc.emit_ph()

    with ExitStack() as pctx:
        PE = pctx.enter_context
        xTc = PE(nc.sbuf_tensor("xTcm", [128, 8, 512], BF16))
        gur = [PE(nc.sbuf_tensor(f"gur{i}", [128, 8, 128], BF16)) for i in range(3)]
        dpr = [PE(nc.sbuf_tensor(f"dpr{i}", [128, 8, D], BF16)) for i in range(2)]
        zbuf = PE(nc.sbuf_tensor("zbuf", [128, 8, 512], BF16))
        zg = PE(nc.sbuf_tensor("zg", [128, 512], BF16))
        accA = PE(nc.sbuf_tensor("accA", [128, 4, D], F32))
        accB = PE(nc.sbuf_tensor("accB", [128, 4, D], F32))
        SF3 = PE(nc.sbuf_tensor("mSF3", [128, D], F32))
        bn6d = PE(nc.sbuf_tensor("bn6d", [128, 12], F32))
        mvd = PE(nc.sbuf_tensor("mvd", [128, 2], F32))
        vepsd = PE(nc.sbuf_tensor("vepsd", [128, 1], F32))
        sdd = PE(nc.sbuf_tensor("sdd", [128, 1], F32))
        rstdd = PE(nc.sbuf_tensor("rstdd", [128, 1], F32))
        OUTD = PE(nc.sbuf_tensor("OUTD", [128, 2, D], F32))

        NCHUNK = NTT // 4
        with ExitStack() as ectx:
            PXT = ectx.enter_context(nc.psum_tensor("PXTz", [128, D], BF16))
            PG = [ectx.enter_context(nc.psum_tensor(f"PGz{i}", [128, 512], F32))
                  for i in range(3)]
            PD = [ectx.enter_context(nc.psum_tensor(f"PDz{i}", [128, D], F32))
                  for i in range(2)]
            pg_tok = {}
            prev_chunk_md = None
            for cc in range(NCHUNK):
                sc.phase(f"moe_m2_{cc}")
                gt0 = cc * 4
                evx = {}
                for j in range(4):
                    for c in range(8):
                        m = sc.tmm(lambda q, j=j, c=c: q.transpose(
                            PXT[:, c * 128:(c + 1) * 128],
                            R[:, gt0 + j, c * 128:(c + 1) * 128], id_s[:]),
                            waits=[evx.get(j - 1)] if c == 0 else [],
                            mark=(c == 7))
                    evx[j] = sc.op("vector", lambda q, j=j: q.tensor_copy(
                        xTc[:, :, j * 128:(j + 1) * 128],
                        PXT[:].rearrange("p (c f) -> p c f", c=8)), waits=[m])

                d_g = {}
                d_d = {}
                acc_toks = {}
                mgk = {}
                prev_md = None
                prev_acc_tok = None
                prev_vz = None
                d_g[(0, 0)] = sc.dma_in("sync", gur[0][:], GUT[0, 0])
                d_g[(0, 1)] = sc.dma_in("scalar", gur[1][:], GUT[0, 8])
                d_d[0] = sc.dma_in("gpsimd", dpr[0][:], DPT[0])
                for e in range(8):
                    v_z = {}
                    s_zg = {}
                    for fi in range(8):
                        for half, ft in ((0, fi), (1, fi + 8)):
                            k = 2 * fi + half
                            slot = k % 3
                            gslot = k % 3
                            if (e, k) not in d_g:
                                d_g[(e, k)] = sc.dma_in(
                                    "sync" if k % 2 == 0 else "scalar",
                                    gur[slot][:], GUT[e, ft],
                                    waits=[mgk.get((e, k - 3))])
                            for c in range(8):
                                mg = sc.tmm(lambda q, c=c, slot=slot, gslot=gslot: q.matmul(
                                    PG[gslot][:], gur[slot][:, c, :], xTc[:, c, :],
                                    start=(c == 0), stop=(c == 7)),
                                    waits=[d_g[(e, k)], evx[3], pg_tok.get((cc, e, k - 3)),
                                           prev_md if (fi == 0 and half == 0) else None,
                                           prev_chunk_md if (e == 0 and k <= 1) else None]
                                          if c == 0 else [],
                                    mark=(c == 7))
                            mgk[(e, k)] = mg
                        s_zg[fi] = sc.op("scalar", lambda q, fi=fi: q.activation(
                            zg[:], PG[(2 * fi) % 3][:], AF.Silu),
                            waits=[mgk[(e, 2 * fi + 1)], v_z.get(fi - 1),
                                   prev_vz if fi == 0 else None])
                        pg_tok[(cc, e, 2 * fi)] = s_zg[fi]
                        v_z[fi] = sc.op("vector", lambda q, fi=fi: q.tensor_tensor(
                            zbuf[:, fi, :], zg[:], PG[(2 * fi + 1) % 3][:], ALU.mult),
                            waits=[s_zg[fi], prev_md if fi == 0 else None])
                        pg_tok[(cc, e, 2 * fi + 1)] = v_z[fi]
                    if e + 1 < 8:
                        d_g[(e + 1, 0)] = sc.dma_in("sync", gur[0][:], GUT[e + 1, 0],
                                                    waits=[mgk[(e, 15)]])
                        d_g[(e + 1, 1)] = sc.dma_in("scalar", gur[1][:], GUT[e + 1, 8],
                                                    waits=[mgk[(e, 13)]])
                        d_d[e + 1] = sc.dma_in("gpsimd", dpr[(e + 1) % 2][:], DPT[e + 1],
                                               waits=[prev_acc_tok])
                        pg_tok[(cc, e + 1, -3)] = pg_tok[(cc, e, 13)]
                        pg_tok[(cc, e + 1, -2)] = pg_tok[(cc, e, 14)]
                        pg_tok[(cc, e + 1, -1)] = pg_tok[(cc, e, 15)]
                    acc_src = accA if e % 2 == 1 else accB
                    acc_dst = accB if e % 2 == 1 else accA
                    for tsub in range(4):
                        ppar = tsub % 2
                        pd_guard = acc_toks.get((e, tsub - 2)) or acc_toks.get((e - 1, tsub + 2))
                        for cs in range(2):
                            for fc in range(8):
                                md = sc.tmm(lambda q, tsub=tsub, cs=cs, fc=fc, ppar=ppar, e=e: q.matmul(
                                    PD[ppar][:, cs * 512:(cs + 1) * 512],
                                    zbuf[:, fc, tsub * 128:(tsub + 1) * 128],
                                    dpr[e % 2][:, fc, cs * 512:(cs + 1) * 512],
                                    start=(fc == 0), stop=(fc == 7)),
                                    waits=([v_z[7], d_d[e], pd_guard]
                                           if (fc == 0 and cs == 0 and tsub == 0) else
                                           ([pd_guard] if (fc == 0 and cs == 0) else [])),
                                    mark=(fc == 7 and cs == 1))
                        gt = gt0 + tsub
                        if e == 0:
                            tok = sc.op("vector", lambda q, tsub=tsub, ppar=ppar, gt=gt: q.tensor_scalar(
                                accA[:, tsub, :], PD[ppar][:], comb[:, gt, 0:1], None, ALU.mult),
                                waits=[md])
                        else:
                            tok = sc.op("vector", lambda q, tsub=tsub, ppar=ppar, gt=gt, e=e,
                                        a=acc_src, b=acc_dst: q.scalar_tensor_tensor(
                                b[:, tsub, :], PD[ppar][:], comb[:, gt, e:e + 1],
                                a[:, tsub, :], ALU.mult, ALU.add),
                                waits=[md])
                        acc_toks[(e, tsub)] = tok
                    prev_acc_tok = tok
                    prev_md = md
                    prev_vz = v_z[7]
                prev_chunk_md = md

                accF = accB
                for tsub in range(4):
                    gt = gt0 + tsub
                    seq, ti = gt // NT, gt % NT
                    sc.op("vector", lambda q, tsub=tsub, gt=gt: q.tensor_tensor(
                        SF3[:], accF[:, tsub, :], R[:, gt, :], ALU.add),
                        waits=[prev_acc_tok])
                    sc.op("vector", lambda q: q.bn_stats(bn6d[:, 0:6], SF3[:, 0:512]))
                    sc.op("vector", lambda q: q.bn_stats(bn6d[:, 6:12], SF3[:, 512:1024]))
                    v_eps = sc.op("vector", lambda q: q.bn_aggr(mvd[:], bn6d[:]))
                    s_sd = sc.op("scalar", lambda q: q.activation(
                        sdd[:], mvd[:, 1:2], AF.Sqrt, bias=io["eps_t"][:]), waits=[v_eps])
                    sc.op("vector", lambda q: q.reciprocal(rstdd[:], sdd[:]), waits=[s_sd])
                    opar = gt % 2
                    sc.op("vector", lambda q, opar=opar: q.tensor_scalar(
                        OUTD[:, opar, :], SF3[:], mvd[:, 0:1], rstdd[:],
                        ALU.subtract, ALU.mult),
                        waits=[io["d_moe_out"].get(gt - 2)])
                    io["d_moe_out"][gt] = sc.dma_out(
                        "sync", out[seq, ti * 128:(ti + 1) * 128, :], OUTD[:, opar, :],
                        waits=[(sc.sem["vector"], sc.sem["vector"].val)])
                sc.drain_outs()
                sc.emit_ph()
    bctx.close()


def build_launch_d():
    nc = bass.Bass("TRN2", target_bir_lowering=False)
    io = {
        "h3in": nc.dram_tensor("h3in", [NSEQ, S, D], F32, kind="ExternalInput").ap(),
        "router_w": nc.dram_tensor("router_w", [8, D], F32, kind="ExternalInput").ap(),
        "gate_up": nc.dram_tensor("gate_up", [8, 2048, D], F32, kind="ExternalInput").ap(),
        "down_p": nc.dram_tensor("down_p", [8, D, D], F32, kind="ExternalInput").ap(),
        "ident": nc.dram_tensor("ident", [128, 128], BF16, kind="ExternalInput").ap(),
        "out": nc.dram_tensor("out", [NSEQ, S, D], F32, kind="ExternalOutput").ap(),
    }
    with ExitStack() as ctx:
        E = ctx.enter_context
        io["R"] = E(nc.sbuf_tensor("R", [128, NSEQ * NT, D], BF16))
        io["id_s"] = E(nc.sbuf_tensor("id_s", [128, 128], BF16))
        io["d_moe_out"] = {}
        io["eps_t"] = E(nc.sbuf_tensor("eps_t", [128, 1], F32))
        sc = Sched(nc, ctx, "d")
        sc.phase("pre")
        sc.op("vector", lambda q: q.memset(io["eps_t"][:], EPS))
        t1 = sc.dma_in("sync", io["id_s"][:], io["ident"][:])
        io["tok_pre"] = [t1]
        sc.emit_ph()
        emit_load_R(sc, io, io["h3in"])
        emit_l1_moe(sc, E, io)
        sc.finalize()
    return nc


def host_l1_moe_inputs(inputs):
    return {
        "router_w": np.ascontiguousarray(np.asarray(inputs["L1_ffn_router_weight"], np.float32)),
        "gate_up": np.ascontiguousarray(np.asarray(inputs["L1_ffn_gate_up_proj"], np.float32)),
        "down_p": np.ascontiguousarray(np.asarray(inputs["L1_ffn_down_proj"], np.float32)),
        "ident": np.eye(128, dtype=ml_dtypes.bfloat16),
    }


def host_l0_attn_inputs(xc, inputs):
    maskq = np.where(np.arange(128)[:, None] >= np.arange(128)[None, :], 0.0,
                     -40000.0).astype(np.float32)
    return {
        "x": np.ascontiguousarray(xc),
        "wq": inputs["L0_attn_w_q"],
        "wk": inputs["L0_attn_w_k"],
        "bq": np.ascontiguousarray(np.asarray(inputs["L0_attn_b_q"]).reshape(QK, 1)),
        "bk": np.ascontiguousarray(np.asarray(inputs["L0_attn_b_k"]).reshape(QK, 1)),
        "maskq": maskq,
        "maskT": np.ascontiguousarray(maskq.T),
        "ident": np.eye(128, dtype=ml_dtypes.bfloat16),
        "ones1": np.ones((1, 128), dtype=ml_dtypes.bfloat16),
    }


def build_launch_a():
    nc = bass.Bass("TRN2", target_bir_lowering=False)
    io = {
        "x": nc.dram_tensor("x", [NSEQ, S, D], F32, kind="ExternalInput").ap(),
        "wq": nc.dram_tensor("wq", [QK, D], F32, kind="ExternalInput").ap(),
        "wk": nc.dram_tensor("wk", [QK, D], F32, kind="ExternalInput").ap(),
        "bq": nc.dram_tensor("bq", [QK, 1], F32, kind="ExternalInput").ap(),
        "bk": nc.dram_tensor("bk", [QK, 1], F32, kind="ExternalInput").ap(),
        "maskq": nc.dram_tensor("maskq", [128, 128], F32, kind="ExternalInput").ap(),
        "maskT": nc.dram_tensor("maskT", [128, 128], F32, kind="ExternalInput").ap(),
        "ident": nc.dram_tensor("ident", [128, 128], BF16, kind="ExternalInput").ap(),
        "ones1": nc.dram_tensor("ones1", [1, 128], BF16, kind="ExternalInput").ap(),
        "h1": nc.dram_tensor("h1", [NSEQ, S, D], F32, kind="ExternalOutput").ap(),
    }
    with ExitStack() as ctx:
        E = ctx.enter_context
        io["R"] = E(nc.sbuf_tensor("R", [128, NSEQ * NT, D], BF16))
        io["id_s"] = E(nc.sbuf_tensor("id_s", [128, 128], BF16))
        io["eps_t"] = E(nc.sbuf_tensor("eps_t", [128, 1], F32))
        sc = Sched(nc, ctx, "a")
        sc.phase("pre")
        sc.op("vector", lambda q: q.memset(io["eps_t"][:], EPS))
        sc.emit_ph()
        emit_l0_attn(sc, E, io)
        sc.finalize()
    return nc



def build_fused():
    nc = bass.Bass("TRN2", target_bir_lowering=False)
    io = {
        "x": nc.dram_tensor("x", [NSEQ, S, D], F32, kind="ExternalInput").ap(),
        "wq": nc.dram_tensor("wq", [QK, D], F32, kind="ExternalInput").ap(),
        "wk": nc.dram_tensor("wk", [QK, D], F32, kind="ExternalInput").ap(),
        "bq": nc.dram_tensor("bq", [QK, 1], F32, kind="ExternalInput").ap(),
        "bk": nc.dram_tensor("bk", [QK, 1], F32, kind="ExternalInput").ap(),
        "maskq": nc.dram_tensor("maskq", [128, 128], F32, kind="ExternalInput").ap(),
        "maskT": nc.dram_tensor("maskT", [128, 128], F32, kind="ExternalInput").ap(),
        "ident": nc.dram_tensor("ident", [128, 128], BF16, kind="ExternalInput").ap(),
        "identf": nc.dram_tensor("identf", [128, 128], F32, kind="ExternalInput").ap(),
        "ones1": nc.dram_tensor("ones1", [1, 128], BF16, kind="ExternalInput").ap(),
        "ones_f": nc.dram_tensor("ones_f", [1, 128], F32, kind="ExternalInput").ap(),
        "ones_col": nc.dram_tensor("ones_col", [128, 1], BF16, kind="ExternalInput").ap(),
        "W_exp": nc.dram_tensor("W_exp", [D, 4096], F32, kind="ExternalInput").ap(),
        "W_con": nc.dram_tensor("W_con", [4096, D], F32, kind="ExternalInput").ap(),
        "b_exp_pm": nc.dram_tensor("b_exp_pm", [128, 32], F32, kind="ExternalInput").ap(),
        "up_bias_pm": nc.dram_tensor("up_bias_pm", [128, 32], F32, kind="ExternalInput").ap(),
        "bfv_up": nc.dram_tensor("bfv_up", [128, 5, 32, 2], F32, kind="ExternalInput").ap(),
        "bfvb_up": nc.dram_tensor("bfvb_up", [128, 5, 4, 16], F32, kind="ExternalInput").ap(),
        "bfv_downT": nc.dram_tensor("bfv_downT", [128, 5, 32, 2], F32, kind="ExternalInput").ap(),
        "SUP": nc.dram_tensor("SUP", [7, 32, 128, 128], F32, kind="ExternalInput").ap(),
        "SUPN": nc.dram_tensor("SUPN", [7, 32, 128, 128], F32, kind="ExternalInput").ap(),
        "SDOWN": nc.dram_tensor("SDOWN", [7, 32, 128, 128], F32, kind="ExternalInput").ap(),
        "SDOWNT": nc.dram_tensor("SDOWNT", [7, 32, 128, 128], F32, kind="ExternalInput").ap(),
        "bcon_row": nc.dram_tensor("bcon_row", [1, D], F32, kind="ExternalInput").ap(),
        "dbias_row": nc.dram_tensor("dbias_row", [1, D], F32, kind="ExternalInput").ap(),
        "Wq": nc.dram_tensor("Wq", [D, D], F32, kind="ExternalInput").ap(),
        "Wk": nc.dram_tensor("Wk", [D, D], F32, kind="ExternalInput").ap(),
        "Wv": nc.dram_tensor("Wv", [D, D], F32, kind="ExternalInput").ap(),
        "Wo": nc.dram_tensor("Wo", [D, D], F32, kind="ExternalInput").ap(),
        "bo_row": nc.dram_tensor("bo_row", [1, D], F32, kind="ExternalInput").ap(),
        "router_w": nc.dram_tensor("router_w", [8, D], F32, kind="ExternalInput").ap(),
        "gate_up": nc.dram_tensor("gate_up", [8, 2048, D], F32, kind="ExternalInput").ap(),
        "down_p": nc.dram_tensor("down_p", [8, D, D], F32, kind="ExternalInput").ap(),
        "out": nc.dram_tensor("out", [NSEQ, S, D], F32, kind="ExternalOutput").ap(),
    }
    with ExitStack() as ctx:
        E = ctx.enter_context
        io["R"] = E(nc.sbuf_tensor("R", [128, NSEQ * NT, D], BF16))
        io["id_s"] = E(nc.sbuf_tensor("id_s", [128, 128], BF16))
        io["id_f"] = E(nc.sbuf_tensor("id_f", [128, 128], F32))
        io["ones_fs"] = E(nc.sbuf_tensor("ones_fsb", [1, 128], F32))
        io["write_inter"] = False
        io["d_ffn_out"] = {}
        io["d_l1_out"] = {}
        io["d_moe_out"] = {}
        io["eps_t"] = E(nc.sbuf_tensor("eps_t", [128, 1], F32))
        sc = Sched(nc, ctx, "f")
        sc.phase("pre")
        sc.op("vector", lambda q: q.memset(io["eps_t"][:], EPS))
        t1 = sc.dma_in("sync", io["id_f"][:], io["identf"][:])
        t2 = sc.dma_in("sync", io["ones_fs"][:], io["ones_f"][:])
        io["tok_pre"] = [t1, t2]
        sc.emit_ph()
        emit_l0_attn(sc, E, io)
        emit_l0_ffn(sc, E, io)
        emit_l1_attn(sc, E, io)
        emit_l1_moe(sc, E, io)
        sc.finalize()
    return nc


def host_fused_inputs(inputs):
    shared = {}
    shared.update(host_l0_ffn_inputs(inputs))
    shared.update(host_l1_attn_inputs(inputs))
    shared.update(host_l1_moe_inputs(inputs))
    shared["ones1"] = np.ones((1, 128), ml_dtypes.bfloat16)
    shared["wq"] = np.ascontiguousarray(np.asarray(inputs["L0_attn_w_q"], np.float32))
    shared["wk"] = np.ascontiguousarray(np.asarray(inputs["L0_attn_w_k"], np.float32))
    shared["bq"] = np.ascontiguousarray(np.asarray(inputs["L0_attn_b_q"], np.float32).reshape(QK, 1))
    shared["bk"] = np.ascontiguousarray(np.asarray(inputs["L0_attn_b_k"], np.float32).reshape(QK, 1))
    maskq = np.where(np.arange(128)[:, None] >= np.arange(128)[None, :], 0.0,
                     -40000.0).astype(np.float32)
    shared["maskq"] = maskq
    shared["maskT"] = np.ascontiguousarray(maskq.T)
    return shared



_CACHE = {}


def kernel(**inputs):
    x = np.asarray(inputs["L0_attn_x"], dtype=np.float32)
    if "fused" not in _CACHE:
        _CACHE["fused"] = build_fused()
    nc = _CACHE["fused"]
    shared = host_fused_inputs(inputs)
    in_maps = []
    for c in range(NCORES):
        d = dict(shared)
        d["x"] = np.ascontiguousarray(x[2 * c:2 * c + 2])
        in_maps.append(d)
    res = run_bass_kernel_spmd(nc, in_maps, core_ids=list(range(NCORES)))
    return np.concatenate([r["out"].reshape(NSEQ, S, D) for r in res.results], axis=0)


# version 17 (final)
# speedup vs baseline: 1.1651x; 1.1651x over previous
from contextlib import ExitStack

import numpy as np
import ml_dtypes

import concourse.bass as bass
import concourse.mybir as mybir
from concourse.bass_utils import run_bass_kernel_spmd

F32 = mybir.dt.float32
BF16 = mybir.dt.bfloat16
AF = mybir.ActivationFunctionType
ALU = mybir.AluOpType
AX = mybir.AxisListType

NCORES = 8
D = 1024
S = 2048
NSEQ = 2
NT = S // 128
QK = 16
EPS = 1e-5



ENGS = ("sync", "gpsimd", "tensor", "vector", "scalar")


class Sem:
    def __init__(self, nc, ctx, name):
        self.h = ctx.enter_context(nc.semaphore(name))
        self.val = 0


class Phase:

    def __init__(self, name):
        self.name = name
        self.steps = {e: [] for e in ENGS}

    def add(self, eng, emit, waits=(), incs=()):
        self.steps[eng].append((tuple(waits), emit, tuple(incs)))

    def emit(self, nc):
        with nc.Block() as blk:
            for eng in ENGS:
                steps = self.steps[eng]
                if not steps:
                    continue

                def body(q, steps=steps):
                    for waits, emit, incs in steps:
                        for (sem, v) in waits:
                            if v > 0:
                                q.wait_ge(sem.h, v)
                        ins = emit(q)
                        for (sem, n) in incs:
                            ins.then_inc(sem.h, n)

                getattr(blk, eng)(body)


class Sched:

    def __init__(self, nc, ctx, prefix):
        self.nc = nc
        self.prefix = prefix
        self.ph = None
        self.sem = {e: Sem(nc, ctx, f"{prefix}_{e}") for e in ENGS}
        self.dmaq = {e: [Sem(nc, ctx, f"{prefix}_dma_{e}{i}") for i in range(4)]
                     for e in ("gpsimd", "sync", "scalar")}
        self.dmac = {e: 0 for e in ("gpsimd", "sync", "scalar")}
        self.doutq = [Sem(nc, ctx, f"{prefix}_dout{i}") for i in range(4)]
        self.doutc = 0

    def phase(self, name):
        assert self.ph is None, "previous phase not emitted"
        self.ph = Phase(f"{self.prefix}_{name}")
        return self.ph

    def emit_ph(self):
        self.ph.emit(self.nc)
        self.ph = None

    def op(self, eng, emit, waits=()):
        s = self.sem[eng]
        w = [(s, s.val)] + [x for x in waits if x is not None]
        s.val += 1
        self.ph.add(eng, emit, w, [(s, 1)])
        return (s, s.val)

    def tmm(self, emit, waits=(), mark=False):
        incs = []
        ret = None
        if mark:
            s = self.sem["tensor"]
            s.val += 1
            incs = [(s, 1)]
            ret = (s, s.val)
        self.ph.add("tensor", emit, [x for x in waits if x is not None], incs)
        return ret

    def dma_in(self, eng, out, in_, waits=()):
        sem = self.dmaq[eng][self.dmac[eng] % 4]
        self.dmac[eng] += 1
        w = [(sem, sem.val)] + [x for x in waits if x is not None]
        sem.val += 16
        self.ph.add(eng, lambda q, o=out, i=in_: q.dma_start(out=o, in_=i),
                    w, [(sem, 16)])
        return (sem, sem.val)

    def dma_marks(self):
        return [(s, s.val) for ring in self.dmaq.values() for s in ring if s.val > 0]

    def dma_out(self, eng, out, in_, waits=()):
        sem = self.doutq[self.doutc % 4]
        self.doutc += 1
        w = [(sem, sem.val)] + [x for x in waits if x is not None]
        sem.val += 16
        self.ph.add(eng, lambda q, o=out, i=in_: q.dma_start(out=o, in_=i),
                    w, [(sem, 16)])
        return (sem, sem.val)

    def drain_outs(self):
        w = [(x, x.val) for x in self.doutq if x.val > 0]
        if w:
            self.ph.add("sync", lambda q: q.nop(), w, ())

    def finalize(self):
        fin = Phase(f"{self.prefix}_fin")
        fin.add("sync", lambda q: q.nop(),
                [(s, s.val) for s in self.doutq if s.val > 0], ())
        fin.emit(self.nc)


def col_chunks(n, c=512):
    out = []
    s = 0
    while s < n:
        w = min(c, n - s)
        out.append((s, w))
        s += w
    return out



def emit_l0_attn(sc: Sched, E, io):
    nc = sc.nc
    x, wq, wk, bq, bk = io["x"], io["wq"], io["wk"], io["bq"], io["bk"]
    maskq, maskT, ident, ones1, h1 = (io["maskq"], io["maskT"], io["ident"],
                                      io["ones1"], io.get("h1"))
    R = io["R"]
    id_s = io["id_s"]
    inter = io.get("write_inter", True)

    bq_s = E(nc.sbuf_tensor("bq_s", [QK, 1], F32))
    bk_s = E(nc.sbuf_tensor("bk_s", [QK, 1], F32))
    mq_s = E(nc.sbuf_tensor("mq_s", [128, 128], F32))
    mT_s = E(nc.sbuf_tensor("mT_s", [128, 128], F32))
    on_s = E(nc.sbuf_tensor("on_s", [1, 128], BF16))
    wq_s = E(nc.sbuf_tensor("wq_s", [QK, D], BF16))
    wk_s = E(nc.sbuf_tensor("wk_s", [QK, D], BF16))
    wqT = E(nc.sbuf_tensor("wqT", [128, 8, QK], BF16))
    wkT = E(nc.sbuf_tensor("wkT", [128, 8, QK], BF16))

    d_o = {}
    with ExitStack() as sctx:
        SE = sctx.enter_context
        qT = SE(nc.sbuf_tensor("qT", [QK, S], BF16))
        kT = SE(nc.sbuf_tensor("kT", [QK, S], BF16))
        xT = SE(nc.sbuf_tensor("xT", [128, 2, 8, 128], BF16))
        m4row = SE(nc.sbuf_tensor("m4row", [1, S], BF16))
        denom = SE(nc.sbuf_tensor("denom", [128, NT], F32))
        mx = SE(nc.sbuf_tensor("mx", [128, 1], F32))
        m4b = SE(nc.sbuf_tensor("m4b", [128, 1], BF16))
        bias1 = SE(nc.sbuf_tensor("bias1", [128, 1], F32))
        dump = SE(nc.sbuf_tensor("dump", [128, S], BF16))
        prob = SE(nc.sbuf_tensor("prob", [128, NT, 128], BF16))
        rec = SE(nc.sbuf_tensor("rec", [128, 1], F32))
        SF = SE(nc.sbuf_tensor("SF", [128, D], F32))
        bn6 = SE(nc.sbuf_tensor("bn6", [128, 12], F32))
        mv = SE(nc.sbuf_tensor("mv", [128, 2], F32))
        veps = SE(nc.sbuf_tensor("veps", [128, 1], F32))
        sd = SE(nc.sbuf_tensor("sd", [128, 1], F32))
        rstd = SE(nc.sbuf_tensor("rstd", [128, 1], F32))
        mnra = SE(nc.sbuf_tensor("mnra", [128, 1], F32))
        OUT = SE(nc.sbuf_tensor("OUT", [128, 2, D], F32))

        for seq in range(NSEQ):
            Roff = seq * NT

            with ExitStack() as pctx:
                PE = pctx.enter_context
                PT = [PE(nc.psum_tensor(f"PT{i}_{seq}", [128, D], BF16)) for i in range(2)]
                QP = [PE(nc.psum_tensor(f"QP{i}_{seq}", [QK, 128], F32)) for i in range(2)]
                KP = [PE(nc.psum_tensor(f"KP{i}_{seq}", [QK, 128], F32)) for i in range(2)]
                WTP = PE(nc.psum_tensor(f"WTP_{seq}", [128, 2 * QK], BF16))

                sc.phase(f"s{seq}_p1")
                sc.dma_in("gpsimd", R[:, Roff:Roff + NT, :],
                          x[seq].rearrange("(t p) d -> p t d", p=128))
                if seq == 0:
                    for src, dst in ((wq, wq_s), (wk, wk_s)):
                        sc.dma_in("gpsimd", dst[:], src[:])
                    for src, dst in ((ident, id_s), (ones1, on_s), (bq, bq_s),
                                     (bk, bk_s), (maskq, mq_s), (maskT, mT_s)):
                        sc.dma_in("sync", dst[:], src[:])
                dma_all = sc.dma_marks()

                if seq == 0:
                    last_ev = None
                    for c in range(8):
                        sc.tmm(lambda q, c=c: q.transpose(
                            WTP[:, 0:QK], wq_s[:, c * 128:(c + 1) * 128],
                            id_s[0:QK, 0:QK]),
                            waits=(dma_all if c == 0 else []) + [last_ev])
                        m = sc.tmm(lambda q, c=c: q.transpose(
                            WTP[:, QK:2 * QK], wk_s[:, c * 128:(c + 1) * 128],
                            id_s[0:QK, 0:QK]),
                            mark=True)
                        sc.op("scalar", lambda q, c=c: q.copy(wqT[:, c, :], WTP[:, 0:QK]),
                              waits=[m])
                        last_ev = sc.op("scalar", lambda q, c=c: q.copy(
                            wkT[:, c, :], WTP[:, QK:2 * QK]))

                v_x = [None] * NT
                s_qk = [None] * NT
                for i in range(NT):
                    par = i % 2
                    for c in range(8):
                        w0 = []
                        if c == 0:
                            w0 = (dma_all if i == 0 else []) + [
                                  v_x[i - 2] if i >= 2 else (last_ev if (seq == 0 and i == 0) else None)]
                        m_tp = sc.tmm(lambda q, i=i, c=c, par=par: q.transpose(
                            PT[par][:, c * 128:(c + 1) * 128],
                            R[:, Roff + i, c * 128:(c + 1) * 128], id_s[:]), waits=w0,
                            mark=(c == 7))
                    v_x[i] = sc.op("vector", lambda q, par=par: q.tensor_copy(
                        xT[:, par, :, :].rearrange("p a b -> p (a b)"), PT[par][:]),
                        waits=[m_tp])
                    for c in range(8):
                        sc.tmm(lambda q, c=c, par=par: q.matmul(
                            QP[par][:], wqT[:, c, :], xT[:, par, c, :],
                            start=(c == 0), stop=(c == 7)),
                            waits=[v_x[i] if c == 0 else None,
                                   s_qk[i - 2] if (c == 0 and i >= 2) else None])
                    for c in range(8):
                        t_qk = sc.tmm(lambda q, c=c, par=par: q.matmul(
                            KP[par][:], wkT[:, c, :], xT[:, par, c, :],
                            start=(c == 0), stop=(c == 7)), mark=(c == 7))
                    sc.op("scalar", lambda q, i=i, par=par: q.activation(
                        qT[:, i * 128:(i + 1) * 128], QP[par][:], AF.Identity, bias=bq_s[:]),
                        waits=[t_qk])
                    s_qk[i] = sc.op("scalar", lambda q, i=i, par=par: q.activation(
                        kT[:, i * 128:(i + 1) * 128], KP[par][:], AF.Identity, bias=bk_s[:]))
                sc.emit_ph()

            with ExitStack() as pctx:
                PE = pctx.enter_context
                SC = PE(nc.psum_tensor(f"SC_{seq}", [128, S], F32))
                MR = PE(nc.psum_tensor(f"MR_{seq}", [1, 128], BF16))

                sc.phase(f"s{seq}_p2")
                s_p2 = [None] * NT
                s_mr = [None] * NT
                m_mr = [None] * NT
                for qi in range(NT):
                    ncols = (qi + 1) * 128
                    chunks = col_chunks(ncols)
                    for j, (st, w) in enumerate(chunks):
                        t_sc = sc.tmm(lambda q, qi=qi, st=st, w=w: q.matmul(
                            SC[:, st:st + w], qT[:, qi * 128:(qi + 1) * 128],
                            kT[:, st:st + w], start=True, stop=True),
                            waits=[s_p2[qi - 1] if (qi >= 1 and st == 0) else None],
                            mark=(j == len(chunks) - 1))
                    sc.op("vector", lambda q, qi=qi: q.tensor_tensor(
                        SC[:, qi * 128:(qi + 1) * 128], SC[:, qi * 128:(qi + 1) * 128],
                        mq_s[:], ALU.add),
                        waits=[t_sc, s_p2[qi - 1] if qi >= 1 else None])
                    sc.op("vector", lambda q, ncols=ncols: q.reduce_max(
                        mx[:], SC[:, 0:ncols], axis=AX.X))
                    sc.op("vector", lambda q: q.tensor_scalar(
                        m4b[:], mx[:], -1.0, None, ALU.mult),
                        waits=[m_mr[qi - 1] if qi >= 1 else None])
                    v_p2 = sc.op("vector", lambda q: q.tensor_scalar(
                        bias1[:], m4b[:], 0.25, None, ALU.mult))
                    s_p2[qi] = sc.op("scalar", lambda q, qi=qi, ncols=ncols: q.activation(
                        dump[:, 0:ncols], SC[:, 0:ncols], AF.Exp, scale=0.25,
                        bias=bias1[:], accum_out=denom[:, qi:qi + 1]), waits=[v_p2])
                    m_mr[qi] = sc.tmm(lambda q: q.transpose(MR[:], m4b[:], id_s[:]),
                                      waits=[v_p2, s_mr[qi - 1] if qi >= 1 else None],
                                      mark=True)
                    s_mr[qi] = sc.op("scalar", lambda q, qi=qi: q.copy(
                        m4row[0:1, qi * 128:(qi + 1) * 128], MR[:]), waits=[m_mr[qi]])
                sc.emit_ph()

            with ExitStack() as pctx:
                PE = pctx.enter_context
                ST = [PE(nc.psum_tensor(f"STp{i}_{seq}", [128, 128], F32)) for i in range(4)]
                PV = PE(nc.psum_tensor(f"PV_{seq}", [128, D], F32))

                sc.phase(f"s{seq}_p3")
                s_exp = {}
                v_evac = {}
                pv_marks = {}
                steps = [(qi, ki) for qi in range(NT - 1, -1, -1) for ki in range(qi + 1)]
                sidx = {sk: i for i, sk in enumerate(steps)}
                DEPTH = 2

                def emit_st(idx):
                    qi, ki = steps[idx]
                    par = idx % 4
                    st_guard = s_exp.get(idx - 4)
                    sc.tmm(lambda q, qi=qi, ki=ki, par=par: q.matmul(
                        ST[par][:], kT[:, ki * 128:(ki + 1) * 128],
                        qT[:, qi * 128:(qi + 1) * 128], start=True, stop=False),
                        waits=[st_guard])
                    t_st = sc.tmm(lambda q, qi=qi, par=par: q.matmul(
                        ST[par][:], on_s[:], m4row[0:1, qi * 128:(qi + 1) * 128],
                        start=False, stop=True), mark=True)
                    wmask = None
                    if ki == qi:
                        wmask = sc.op("vector", lambda q, par=par: q.tensor_tensor(
                            ST[par][:], ST[par][:], mT_s[:], ALU.add), waits=[t_st])
                    pguard = pv_marks.get((qi + 1, ki))
                    s_exp[idx] = sc.op("scalar", lambda q, ki=ki, par=par: q.activation(
                        prob[:, ki, :], ST[par][:], AF.Exp, scale=0.25),
                        waits=[t_st, wmask, pguard])

                def emit_pv(idx):
                    qi, ki = steps[idx]
                    for (st, w) in ((0, 512), (512, 512)):
                        pv_marks[(qi, ki)] = sc.tmm(lambda q, qi=qi, ki=ki, st=st, w=w: q.matmul(
                            PV[:, st:st + w], prob[:, ki, :],
                            R[:, Roff + ki, st:st + w],
                            start=(ki == 0), stop=(ki == qi)),
                            waits=[s_exp[idx] if st == 0 else None,
                                   v_evac.get(qi + 1) if (ki == 0 and st == 0) else None],
                            mark=(st == 512))
                    if ki == qi:
                        sc.op("vector", lambda q, qi=qi: q.reciprocal(
                            rec[:], denom[:, qi:qi + 1]), waits=[pv_marks[(qi, qi)]])
                        sc.op("vector", lambda q, qi=qi: q.scalar_tensor_tensor(
                            SF[:], PV[:], rec[:], R[:, Roff + qi, :], ALU.mult, ALU.add))
                        v_evac[qi] = sc.op("vector", lambda q, qi=qi: q.tensor_copy(
                            R[:, Roff + qi, :], SF[:]))

                for idx in range(len(steps) + DEPTH):
                    if idx - DEPTH >= 0:
                        emit_pv(idx - DEPTH)
                    if idx < len(steps):
                        emit_st(idx)
                sc.emit_ph()

            sc.phase(f"s{seq}_p4")
            s_sd = [None] * NT
            v_rec = [None] * NT
            g_nrm = [None] * NT
            for i in range(NT):
                gi = seq * NT + i
                par = i % 2
                sc.op("vector", lambda q, i=i: q.bn_stats(bn6[:, 0:6], R[:, Roff + i, 0:512]),
                      waits=[s_sd[i - 1] if i >= 1 else None,
                             g_nrm[i - 1] if i >= 1 else None])
                sc.op("vector", lambda q, i=i: q.bn_stats(bn6[:, 6:12], R[:, Roff + i, 512:1024]))
                v_st = sc.op("vector", lambda q: q.bn_aggr(mv[:], bn6[:]))
                s_sd[i] = sc.op("scalar", lambda q: q.activation(
                    sd[:], mv[:, 1:2], AF.Sqrt, bias=io["eps_t"][:]),
                    waits=[v_st, v_rec[i - 1] if i >= 1 else None])
                v_rec[i] = sc.op("vector", lambda q: q.reciprocal(rstd[:], sd[:]),
                                 waits=[s_sd[i]])
                if inter:
                    sc.op("vector", lambda q, i=i, par=par: q.tensor_scalar(
                        OUT[:, par, :], R[:, Roff + i, :], mv[:, 0:1], rstd[:],
                        ALU.subtract, ALU.mult),
                        waits=[d_o.get(gi - 2)])
                    g_nrm[i] = sc.op("vector", lambda q, i=i, par=par: q.tensor_copy(
                        R[:, Roff + i, :], OUT[:, par, :]))
                    d_o[gi] = sc.dma_out("sync", h1[seq, i * 128:(i + 1) * 128, :],
                                         OUT[:, par, :], waits=[g_nrm[i]])
                else:
                    v_mnr = sc.op("vector", lambda q: q.tensor_scalar(
                        mnra[:], mv[:, 0:1], rstd[:], -1.0, ALU.mult, ALU.mult))
                    g_nrm[i] = sc.op("gpsimd", lambda q, i=i: q.tensor_scalar(
                        R[:, Roff + i, :], R[:, Roff + i, :], rstd[:], mnra[:],
                        ALU.mult, ALU.add), waits=[v_mnr])
            if seq == NSEQ - 1:
                sc.drain_outs()
            sc.emit_ph()




def emit_load_R(sc: Sched, io, src_ap):
    sc.phase("loadR")
    R = io["R"]
    for seq in range(NSEQ):
        sc.dma_in("gpsimd", R[:, seq * NT:(seq + 1) * NT, :],
                  src_ap[seq].rearrange("(t p) d -> p t d", p=128))
    tok = sc.dma_marks()
    sc.ph.add("gpsimd", lambda q: q.nop(), tok, ())
    sc.emit_ph()
    return tok


def emit_l0_ffn(sc: Sched, E, io):
    nc = sc.nc
    R, id_s = io["R"], io["id_s"]
    h2 = io.get("h2")
    inter = io.get("write_inter", True)
    ones_f = io["ones_fs"]

    bctx = ExitStack()
    E = bctx.enter_context
    GV = E(nc.sbuf_tensor("GV", [128, 32, D], BF16))
    gb = E(nc.sbuf_tensor("gb", [128, 32], BF16))
    gbias = E(nc.sbuf_tensor("gbias", [128, 32], F32))
    bexp_pm = E(nc.sbuf_tensor("bexp_pm", [128, 32], F32))
    ubias_pm = E(nc.sbuf_tensor("ubias_pm", [128, 32], F32))
    bfv_u = E(nc.sbuf_tensor("bfv_u", [128, 5 * 32 * 2], F32))
    bfv_d = E(nc.sbuf_tensor("bfv_d", [128, 5 * 32 * 2], F32))
    bfvb_u = E(nc.sbuf_tensor("bfvb_u", [128, 5, 4, 16], F32))
    bias_tmp = E(nc.sbuf_tensor("bias_tmp", [128, 64], F32))
    bias_bc = E(nc.sbuf_tensor("bias_bc", [128, D], F32))

    W1H = nc.dram_tensor("W1H", [32, 128, 8, 128], BF16, kind="Internal").ap()

    def bfv_col(buf, st, t, which):
        idx = ((st - 7) * 32 + t) * 2 + which
        return buf[:, idx:idx + 1]

    def vec_stages(buf, stages, swap_unused, first_wait):
        tmp1 = buf["tmp1"]; tmp2 = buf["tmp2"]; bfv = buf["bfv"]
        fw = list(first_wait or [])
        v_last = None
        for sti, st in enumerate(stages):
            hs = 1 << (st - 7)
            for ta in range(32):
                if ta & hs:
                    continue
                tb = ta + hs
                sc.op("vector", lambda q, st=st, ta=ta, tb=tb: q.tensor_scalar(
                    tmp1[:], GV[:, tb, :], bfv_col(bfv, st, ta, 1), None, ALU.mult),
                    waits=fw)
                fw = []
                sc.op("vector", lambda q, st=st, ta=ta, tb=tb: q.tensor_scalar(
                    tmp2[:], GV[:, ta, :], bfv_col(bfv, st, tb, 1), None, ALU.mult))
                sc.op("vector", lambda q, st=st, ta=ta, tb=tb: q.scalar_tensor_tensor(
                    GV[:, tb, :], GV[:, tb, :], bfv_col(bfv, st, tb, 0), tmp2[:],
                    ALU.mult, ALU.add))
                v_last = sc.op("vector", lambda q, st=st, ta=ta: q.scalar_tensor_tensor(
                    GV[:, ta, :], GV[:, ta, :], bfv_col(bfv, st, ta, 0), tmp1[:],
                    ALU.mult, ALU.add))
            if buf.get("with_bias"):
                bb = buf["bfvb"]
                bt = buf["bias_tmp"]
                nb_ = 32 // (2 * hs)
                gav = gb.rearrange("p (a two b) -> p a two b", two=2, b=hs)

                def bslice(k, sti=sti, bb=bb, nb_=nb_, hs=hs):
                    return bb[:, sti, k, :].rearrange("p (a b) -> p a b", b=hs)

                sc.op("vector", lambda q, gav=gav, sti=sti: q.tensor_tensor(
                    bt[:, 0:16].rearrange("p (a b) -> p a b", b=hs),
                    gav[:, :, 0, :], bslice(0), ALU.mult))
                sc.op("vector", lambda q, gav=gav, sti=sti: q.tensor_tensor(
                    bt[:, 16:32].rearrange("p (a b) -> p a b", b=hs),
                    gav[:, :, 1, :], bslice(1), ALU.mult))
                sc.op("vector", lambda q, gav=gav, sti=sti: q.tensor_tensor(
                    bt[:, 32:48].rearrange("p (a b) -> p a b", b=hs),
                    gav[:, :, 0, :], bslice(2), ALU.mult))
                sc.op("vector", lambda q, gav=gav, sti=sti: q.tensor_tensor(
                    bt[:, 48:64].rearrange("p (a b) -> p a b", b=hs),
                    gav[:, :, 1, :], bslice(3), ALU.mult))
                sc.op("vector", lambda q, gav=gav: q.tensor_tensor(
                    gav[:, :, 0, :],
                    bt[:, 0:16].rearrange("p (a b) -> p a b", b=hs),
                    bt[:, 16:32].rearrange("p (a b) -> p a b", b=hs), ALU.add))
                v_last = sc.op("vector", lambda q, gav=gav: q.tensor_tensor(
                    gav[:, :, 1, :],
                    bt[:, 32:48].rearrange("p (a b) -> p a b", b=hs),
                    bt[:, 48:64].rearrange("p (a b) -> p a b", b=hs), ALU.add))
        return v_last

    def merged_stages(init_ap, chain_aps, with_bias, first_wait, tag):
        with ExitStack() as tctx:
            TE = tctx.enter_context
            sbl = TE(nc.sbuf_tensor(f"sblm{tag}", [128, 2, 32, 128], BF16))
            Cb = TE(nc.sbuf_tensor(f"cbm{tag}", [128, 32, 128], BF16))
            CP = [TE(nc.psum_tensor(f"cpm{tag}{i}", [128, 128], F32)) for i in range(2)]
            GP = [TE(nc.psum_tensor(f"gpm{tag}{i}", [128, D], F32)) for i in range(2)]

            d_init = sc.dma_in("gpsimd", Cb[:], init_ap.rearrange("t j i -> j t i"))
            d_sb = {}
            d_sb[0] = sc.dma_in("gpsimd", sbl[:, 0, :, :], chain_aps[0].rearrange("t j i -> j t i"))
            if len(chain_aps) > 1:
                d_sb[1] = sc.dma_in("gpsimd", sbl[:, 1, :, :], chain_aps[1].rearrange("t j i -> j t i"))
            ev = {}
            for si in range(len(chain_aps)):
                par = si % 2
                for t in range(32):
                    m = sc.tmm(lambda q, si=si, t=t, par=par: q.matmul(
                        CP[t % 2][:], sbl[:, par, t, :], Cb[:, t, :],
                        start=True, stop=True),
                        waits=[d_init if (si == 0 and t == 0) else None,
                               d_sb[si] if t == 0 else None,
                               ev.get((si - 1, t)), ev.get((si, t - 2)),
                               ev.get((si - 1, 30 + t)) if t <= 1 else None],
                        mark=True)
                    eng = "scalar" if t % 2 == 0 else "vector"
                    if eng == "scalar":
                        ev[(si, t)] = sc.op("scalar", lambda q, t=t: q.copy(
                            Cb[:, t, :], CP[t % 2][:]), waits=[m])
                    else:
                        ev[(si, t)] = sc.op("vector", lambda q, t=t: q.tensor_copy(
                            Cb[:, t, :], CP[t % 2][:]), waits=[m])
                if si + 2 < len(chain_aps):
                    d_sb[si + 2] = sc.dma_in("gpsimd", sbl[:, par, :, :],
                                             chain_aps[si + 2].rearrange("t j i -> j t i"),
                                             waits=[m])
            last_si = len(chain_aps) - 1
            ga = {}
            mb_last = None
            for t in range(32):
                for (cs, wdt) in ((0, 512), (512, 512)):
                    ma = sc.tmm(lambda q, t=t, cs=cs: q.matmul(
                        GP[t % 2][:, cs:cs + 512], Cb[:, t, :], GV[:, t, cs:cs + 512],
                        start=True, stop=True),
                        waits=[first_wait if (t == 0 and cs == 0) else None,
                               ev.get((last_si, t)) if cs == 0 else None,
                               ga.get(t - 2) if cs == 0 else None],
                        mark=(cs == 512))
                if with_bias:
                    mb_last = sc.tmm(lambda q, t=t: q.matmul(
                        CP[0][:, t:t + 1], Cb[:, t, :], gb[:, t:t + 1],
                        start=True, stop=True),
                        waits=[ev.get((last_si, 30)) if t == 0 else None,
                               ev.get((last_si, 31)) if t == 0 else None],
                        mark=True)
                eng = "scalar" if t % 2 == 0 else "vector"
                wl = [ma if not with_bias else mb_last]
                if eng == "scalar":
                    ga[t] = sc.op("scalar", lambda q, t=t: q.copy(
                        GV[:, t, :], GP[t % 2][:]), waits=wl)
                else:
                    ga[t] = sc.op("vector", lambda q, t=t: q.tensor_copy(
                        GV[:, t, :], GP[t % 2][:]), waits=wl)
            if with_bias:
                bg = sc.op("vector", lambda q: q.tensor_copy(gb[:], CP[0][:, 0:32]),
                           waits=[mb_last])
                return ga[31], bg
            return ga[31], None

    with ExitStack() as pctx:
        PE = pctx.enter_context
        WEXc = PE(nc.sbuf_tensor("WEXc", [128, 2, 4096], BF16))
        PTW = [PE(nc.psum_tensor(f"PTWu{i}", [128, D], BF16)) for i in range(2)]

        sc.phase("ffn_b1")
        d_bex = sc.dma_in("sync", bexp_pm[:], io["b_exp_pm"][:])
        d_ub = sc.dma_in("sync", ubias_pm[:], io["up_bias_pm"][:])
        d_bu = sc.dma_in("sync", bfv_u[:], io["bfv_up"].rearrange("p a b c -> p (a b c)"))
        d_bd = sc.dma_in("sync", bfv_d[:], io["bfv_downT"].rearrange("p a b c -> p (a b c)"))
        d_bb = sc.dma_in("sync", bfvb_u[:], io["bfvb_up"][:])
        d_on = sc.dma_in("sync", ones_f[:], io["ones_f"][:])
        io["_tok"] = {"d_ub": d_ub, "d_bu": d_bu, "d_bd": d_bd, "d_on": d_on,
                      "d_bb": d_bb}
        v_gb = sc.op("vector", lambda q: q.tensor_copy(gb[:], bexp_pm[:]), waits=[d_bex])

        d_wc = {}
        d_wc[0] = sc.dma_in("gpsimd", WEXc[:, 0, :], io["W_exp"][0:128, :])
        d_wc[1] = sc.dma_in("gpsimd", WEXc[:, 1, :], io["W_exp"][128:256, :])
        ev = {}
        for c in range(8):
            par = c % 2
            for g in range(4):
                gpar = g % 2
                for tt in range(8):
                    t = g * 8 + tt
                    m = sc.tmm(lambda q, t=t, tt=tt, c=c, par=par, gpar=gpar: q.transpose(
                        PTW[gpar][:, tt * 128:(tt + 1) * 128],
                        WEXc[:, par, t * 128:(t + 1) * 128], id_s[:]),
                        waits=[d_wc[c] if (t == 0 and tt == 0) else None,
                               (ev.get((c - 1, 3)) if g == 0 else ev.get((c, g - 2)))
                               if tt == 0 else None] +
                              (io.get("tok_pre", []) if (c == 0 and t == 0) else []),
                        mark=(tt == 7))
                ev[(c, g)] = sc.op("vector", lambda q, g=g, c=c, gpar=gpar: q.tensor_copy(
                    GV[:, g * 8:(g + 1) * 8, c * 128:(c + 1) * 128],
                    PTW[gpar][:].rearrange("p (a b) -> p a b", a=8)), waits=[m])
            if c + 2 < 8:
                d_wc[c + 2] = sc.dma_in("gpsimd", WEXc[:, par, :],
                                        io["W_exp"][(c + 2) * 128:(c + 3) * 128, :],
                                        waits=[ev[(c, 3)]])
        ev_last = ev[(7, 3)]
        merged_stages(io["SUP"][6], [io["SUPN"][st] for st in range(5, -1, -1)],
                      True, ev_last, "u")
        sc.emit_ph()

    with ExitStack() as pctx:
        PE = pctx.enter_context
        tmp1 = PE(nc.sbuf_tensor("tmp1u", [128, D], BF16))
        tmp2 = PE(nc.sbuf_tensor("tmp2u", [128, D], BF16))
        WST = [PE(nc.sbuf_tensor(f"WSTu{i}", [128, 8, 128], BF16)) for i in range(2)]
        PTW = [PE(nc.psum_tensor(f"PTXu{i}", [128, D], BF16)) for i in range(2)]

        sc.phase("ffn_b1v")
        vec_stages({"tmp1": tmp1, "tmp2": tmp2, "bfv": bfv_u, "bfvb": bfvb_u,
                    "bias_tmp": bias_tmp, "with_bias": True}, range(7, 12), None,
                   [io["_tok"]["d_bu"], io["_tok"]["d_bb"]])
        v_gbias = sc.op("vector", lambda q: q.scalar_tensor_tensor(
            gbias[:], gb[:], 1.0, ubias_pm[:], ALU.mult, ALU.add),
            waits=[io["_tok"]["d_ub"]])

        dW = {}
        for t in range(32):
            par = t % 2
            for c in range(8):
                m = sc.tmm(lambda q, t=t, c=c, par=par: q.transpose(
                    PTW[par][:, c * 128:(c + 1) * 128],
                    GV[:, t, c * 128:(c + 1) * 128], id_s[:]),
                    waits=[(v_gbias if t == 0 else dW.get(t - 2)) if c == 0 else None],
                    mark=(c == 7))
            vv = sc.op("vector", lambda q, t=t, par=par: q.tensor_copy(
                WST[par][:].rearrange("p c f -> p (c f)"), PTW[par][:]),
                waits=[m, dW.get(t - 2)])
            dW[t] = sc.dma_out("sync", W1H[t], WST[par][:], waits=[vv])
        sc.drain_outs()
        sc.emit_ph()
    w1h_done = [dW[t] for t in range(32)]

    with ExitStack() as pctx:
        PE = pctx.enter_context
        tmp1 = PE(nc.sbuf_tensor("tmp1d", [128, D], BF16))
        tmp2 = PE(nc.sbuf_tensor("tmp2d", [128, D], BF16))
        brow = PE(nc.sbuf_tensor("brow", [1, D], F32))
        brow2 = PE(nc.sbuf_tensor("brow2", [1, D], F32))
        PBB = PE(nc.psum_tensor("PBBd", [128, D], F32))

        sc.phase("ffn_b2")
        d_wcon = sc.dma_in("gpsimd", GV[:], io["W_con"].rearrange("(t p) d -> p t d", p=128))
        d_bc1 = sc.dma_in("sync", brow[:], io["bcon_row"][:])
        d_bc2 = sc.dma_in("sync", brow2[:], io["dbias_row"][:])
        d_bc = sc.op("vector", lambda q: q.tensor_tensor(
            brow[:], brow[:], brow2[:], ALU.add), waits=[d_bc1, d_bc2])

        v_last = vec_stages({"tmp1": tmp1, "tmp2": tmp2, "bfv": bfv_d},
                            range(11, 6, -1), None, [d_wcon, io["_tok"]["d_bd"]])
        t_last, _ = merged_stages(io["SDOWN"][0], [io["SDOWNT"][st] for st in range(1, 7)],
                                  False, v_last, "d")

        for (cs, wdt) in ((0, 512), (512, 512)):
            sc.tmm(lambda q, cs=cs, wdt=wdt: q.matmul(
                PBB[:, cs:cs + wdt], ones_f[:], brow[:, cs:cs + wdt],
                start=True, stop=True),
                waits=[d_bc, t_last, io["_tok"]["d_on"]] if cs == 0 else [],
                mark=(cs == 512))
        sc.op("vector", lambda q: q.tensor_copy(bias_bc[:], PBB[:]),
              waits=[(sc.sem["tensor"], sc.sem["tensor"].val)])
        sc.emit_ph()

    with ExitStack() as pctx:
        PE = pctx.enter_context
        xTc = PE(nc.sbuf_tensor("xTc", [128, 8, 512], BF16))
        h3c = PE(nc.sbuf_tensor("h3c", [128, 32, 512], BF16))
        wr = PE(nc.sbuf_tensor("wr", [128, 3, 8, 128], BF16))
        SF2 = PE(nc.sbuf_tensor("SF2", [128, D], F32))
        SF3 = PE(nc.sbuf_tensor("SF3", [128, D], F32))
        bn6b = PE(nc.sbuf_tensor("bn6b", [128, 12], F32))
        mvb = PE(nc.sbuf_tensor("mvb", [128, 2], F32))
        vepsb = PE(nc.sbuf_tensor("vepsb", [128, 1], F32))
        sdb = PE(nc.sbuf_tensor("sdb", [128, 1], F32))
        rstdb = PE(nc.sbuf_tensor("rstdb", [128, 1], F32))
        OUTB = PE(nc.sbuf_tensor("OUTB", [128, 2, D], F32))
        PX = [PE(nc.psum_tensor(f"PXf{i}", [128, D], BF16)) for i in range(2)]
        PH = [PE(nc.psum_tensor(f"PHf{i}", [128, 512], F32)) for i in range(2)]
        PY = [PE(nc.psum_tensor(f"PYf{i}", [128, 512], F32)) for i in range(2)]

        NCHUNK = NSEQ * S // 512
        sc.phase("ffn_b3")
        sc.ph.add("sync", lambda q: q.nop(), w1h_done, ())
        evx = {}
        s_h3 = {}
        mh = {}
        d_w = {}
        my_marks = {}
        for gft in range(3):
            d_w[gft] = sc.dma_in("sync", wr[:, gft % 3, :, :], W1H[gft % 32])
        for cc in range(NCHUNK):
            gt0 = cc * 4
            for j in range(4):
                gtj = gt0 + j
                par = gtj % 2
                for c in range(8):
                    m = sc.tmm(lambda q, gtj=gtj, c=c, par=par: q.transpose(
                        PX[par][:, c * 128:(c + 1) * 128],
                        R[:, gtj, c * 128:(c + 1) * 128], id_s[:]),
                        waits=[evx.get(gtj - 2),
                               mh.get((cc - 1, 31)) if j == 0 else None]
                              if c == 0 else [],
                        mark=(c == 7))
                evx[gtj] = sc.op("vector", lambda q, j=j, par=par: q.tensor_copy(
                    xTc[:, :, j * 128:(j + 1) * 128], PX[par][:].rearrange(
                        "p (c f) -> p c f", c=8)), waits=[m])
            for ft in range(32):
                gft = cc * 32 + ft
                par = gft % 2
                for c in range(8):
                    mh[(cc, ft)] = sc.tmm(lambda q, cc=cc, ft=ft, c=c, par=par, gft=gft: q.matmul(
                        PH[par][:], wr[:, gft % 3, c, :], xTc[:, c, :],
                        start=(c == 0), stop=(c == 7)),
                        waits=([d_w[gft], evx[gt0 + 3], s_h3.get(gft - 2),
                                my_marks.get(cc - 1) if ft == 0 else None]
                               if c == 0 else []),
                        mark=(c == 7))
                s_h3[gft] = sc.op("scalar", lambda q, ft=ft, par=par: q.activation(
                    h3c[:, ft, :], PH[par][:], AF.Gelu, bias=gbias[:, ft:ft + 1]),
                    waits=[mh[(cc, ft)]])
                if gft + 3 < NCHUNK * 32:
                    d_w[gft + 3] = sc.dma_in("sync", wr[:, (gft + 3) % 3, :, :],
                                             W1H[(gft + 3) % 32], waits=[mh[(cc, ft)]])
            for tsub in range(4):
                gt = gt0 + tsub
                seq, i = gt // NT, gt % NT
                ppar = tsub % 2
                ev_y = None
                for dh in range(2):
                    for ft in range(32):
                        my = sc.tmm(lambda q, ft=ft, dh=dh, tsub=tsub, ppar=ppar: q.matmul(
                            PY[ppar][:], h3c[:, ft, tsub * 128:(tsub + 1) * 128],
                            GV[:, ft, dh * 512:(dh + 1) * 512],
                            start=(ft == 0), stop=(ft == 31)),
                            waits=([s_h3[cc * 32 + 31], io.get("ev_y_prev")]
                                   if (ft == 0 and dh == 0) else []),
                            mark=(ft == 31))
                    ev_y = sc.op("vector", lambda q, dh=dh, ppar=ppar: q.tensor_tensor(
                        SF2[:, dh * 512:(dh + 1) * 512], PY[ppar][:],
                        bias_bc[:, dh * 512:(dh + 1) * 512], ALU.add), waits=[my])
                    if dh == 0:
                        sc.tmm(lambda q: q.nop(), waits=[ev_y])
                io["ev_y_prev"] = ev_y
                my_marks[cc] = my
                sc.op("vector", lambda q, gt=gt: q.tensor_tensor(
                    SF3[:], SF2[:], R[:, gt, :], ALU.add))
                sc.op("vector", lambda q: q.bn_stats(bn6b[:, 0:6], SF3[:, 0:512]))
                sc.op("vector", lambda q: q.bn_stats(bn6b[:, 6:12], SF3[:, 512:1024]))
                v_eps = sc.op("vector", lambda q: q.bn_aggr(mvb[:], bn6b[:]))
                s_sd = sc.op("scalar", lambda q: q.activation(
                    sdb[:], mvb[:, 1:2], AF.Sqrt, bias=io["eps_t"][:]), waits=[v_eps])
                sc.op("vector", lambda q: q.reciprocal(rstdb[:], sdb[:]), waits=[s_sd])
                opar = gt % 2
                if inter:
                    sc.op("vector", lambda q, opar=opar: q.tensor_scalar(
                        OUTB[:, opar, :], SF3[:], mvb[:, 0:1], rstdb[:],
                        ALU.subtract, ALU.mult),
                        waits=[io["d_ffn_out"].get(gt - 2)])
                    v_r = sc.op("vector", lambda q, gt=gt, opar=opar: q.tensor_copy(
                        R[:, gt, :], OUTB[:, opar, :]))
                    io["d_ffn_out"][gt] = sc.dma_out(
                        "sync", h2[seq, i * 128:(i + 1) * 128, :], OUTB[:, opar, :],
                        waits=[v_r])
                else:
                    sc.op("vector", lambda q, gt=gt: q.tensor_scalar(
                        R[:, gt, :], SF3[:], mvb[:, 0:1], rstdb[:],
                        ALU.subtract, ALU.mult))
        sc.drain_outs()
        sc.emit_ph()
    bctx.close()


def host_l0_ffn_inputs(inputs):
    up_w = np.asarray(inputs["L0_ffn_up_weights"], np.float32)
    dn_w = np.asarray(inputs["L0_ffn_down_weights"], np.float32)
    b_exp = np.asarray(inputs["L0_ffn_b_exp"], np.float32)
    up_bias = np.asarray(inputs["L0_ffn_up_bias"], np.float32)
    b_con = np.asarray(inputs["L0_ffn_b_con"], np.float32)
    d_bias = np.asarray(inputs["L0_ffn_down_bias"], np.float32)

    def stage_dense(w_s, s, transpose_lhs):
        stride = 1 << s
        f = np.arange(4096)
        pair = (f >> (s + 1)) * stride + (f & (stride - 1))
        a = (f & stride) == 0
        diagv = np.where(a, w_s[pair, 0, 0], w_s[pair, 1, 1])
        offv = np.where(a, w_s[pair, 0, 1], w_s[pair, 1, 0])
        partner = f ^ stride
        out = np.zeros((32, 128, 128), np.float32)
        for t in range(32):
            sl = slice(128 * t, 128 * t + 128)
            i_loc = np.arange(128)
            j_off = partner[sl] - 128 * t
            if transpose_lhs:
                out[t][i_loc, i_loc] = diagv[sl]
                out[t][j_off, i_loc] = offv[sl]
            else:
                out[t][i_loc, i_loc] = diagv[sl]
                out[t][i_loc, j_off] = offv[sl]
        return out

    SUP = np.stack([stage_dense(up_w[s], s, True) for s in range(7)])
    SUPN = np.stack([stage_dense(up_w[s], s, False) for s in range(7)])
    SDOWN = np.stack([stage_dense(dn_w[s], s, False) for s in range(7)])
    SDOWNT = np.stack([stage_dense(dn_w[s], s, True) for s in range(7)])

    def bfv(w_all, swap):
        arr = np.zeros((128, 5, 32, 2), np.float32)
        for s in range(7, 12):
            stride = 1 << s
            f = np.arange(4096)
            pair = (f >> (s + 1)) * stride + (f & (stride - 1))
            a = (f & stride) == 0
            w_s = w_all[s]
            diagv = np.where(a, w_s[pair, 0, 0], w_s[pair, 1, 1])
            if not swap:
                offv = np.where(a, w_s[pair, 0, 1], w_s[pair, 1, 0])
            else:
                offv = np.where(a, w_s[pair, 1, 0], w_s[pair, 0, 1])
            for t in range(32):
                sl = slice(128 * t, 128 * t + 128)
                arr[:, s - 7, t, 0] = diagv[sl]
                arr[:, s - 7, t, 1] = offv[sl]
        return arr

    def bfvb(w_all):
        arr = np.zeros((128, 5, 4, 16), np.float32)
        for s_ in range(7, 12):
            stride = 1 << s_
            hs = 1 << (s_ - 7)
            f = np.arange(4096)
            pair = (f >> (s_ + 1)) * stride + (f & (stride - 1))
            a = (f & stride) == 0
            w_s = w_all[s_]
            diagv = np.where(a, w_s[pair, 0, 0], w_s[pair, 1, 1])
            offv = np.where(a, w_s[pair, 0, 1], w_s[pair, 1, 0])
            idx = 0
            for ta in range(32):
                if ta & hs:
                    continue
                tb = ta + hs
                sla = slice(128 * ta, 128 * ta + 128)
                slb = slice(128 * tb, 128 * tb + 128)
                arr[:, s_ - 7, 0, idx] = diagv[sla]
                arr[:, s_ - 7, 1, idx] = offv[sla]
                arr[:, s_ - 7, 2, idx] = offv[slb]
                arr[:, s_ - 7, 3, idx] = diagv[slb]
                idx += 1
        return arr

    def pm(v):
        return np.ascontiguousarray(v.reshape(-1, 128).T)

    return {
        "W_exp": np.ascontiguousarray(np.asarray(inputs["L0_ffn_W_exp"], np.float32)),
        "W_con": np.ascontiguousarray(np.asarray(inputs["L0_ffn_W_con"], np.float32)),
        "b_exp_pm": pm(b_exp),
        "up_bias_pm": pm(up_bias),
        "bfv_up": bfv(up_w, False),
        "bfvb_up": bfvb(up_w),
        "bfv_downT": bfv(dn_w, True),
        "SUP": SUP,
        "SUPN": SUPN,
        "SDOWN": SDOWN,
        "SDOWNT": SDOWNT,
        "bcon_row": np.ascontiguousarray(b_con.reshape(1, D)),
        "dbias_row": np.ascontiguousarray(d_bias.reshape(1, D)),
    }


def build_launch_b():
    nc = bass.Bass("TRN2", target_bir_lowering=False)
    io = {
        "h1": nc.dram_tensor("h1", [NSEQ, S, D], F32, kind="ExternalInput").ap(),
        "W_exp": nc.dram_tensor("W_exp", [D, 4096], F32, kind="ExternalInput").ap(),
        "W_con": nc.dram_tensor("W_con", [4096, D], F32, kind="ExternalInput").ap(),
        "b_exp_pm": nc.dram_tensor("b_exp_pm", [128, 32], F32, kind="ExternalInput").ap(),
        "up_bias_pm": nc.dram_tensor("up_bias_pm", [128, 32], F32, kind="ExternalInput").ap(),
        "bfv_up": nc.dram_tensor("bfv_up", [128, 5, 32, 2], F32, kind="ExternalInput").ap(),
        "bfvb_up": nc.dram_tensor("bfvb_up", [128, 5, 4, 16], F32, kind="ExternalInput").ap(),
        "bfv_downT": nc.dram_tensor("bfv_downT", [128, 5, 32, 2], F32, kind="ExternalInput").ap(),
        "SUP": nc.dram_tensor("SUP", [7, 32, 128, 128], F32, kind="ExternalInput").ap(),
        "SUPN": nc.dram_tensor("SUPN", [7, 32, 128, 128], F32, kind="ExternalInput").ap(),
        "SDOWN": nc.dram_tensor("SDOWN", [7, 32, 128, 128], F32, kind="ExternalInput").ap(),
        "SDOWNT": nc.dram_tensor("SDOWNT", [7, 32, 128, 128], F32, kind="ExternalInput").ap(),
        "bcon_row": nc.dram_tensor("bcon_row", [1, D], F32, kind="ExternalInput").ap(),
        "dbias_row": nc.dram_tensor("dbias_row", [1, D], F32, kind="ExternalInput").ap(),
        "ones_f": nc.dram_tensor("ones_f", [1, 128], F32, kind="ExternalInput").ap(),
        "ident": nc.dram_tensor("ident", [128, 128], BF16, kind="ExternalInput").ap(),
        "h2": nc.dram_tensor("h2", [NSEQ, S, D], F32, kind="ExternalOutput").ap(),
    }
    with ExitStack() as ctx:
        E = ctx.enter_context
        io["R"] = E(nc.sbuf_tensor("R", [128, NSEQ * NT, D], BF16))
        io["id_s"] = E(nc.sbuf_tensor("id_s", [128, 128], BF16))
        io["ones_fs"] = E(nc.sbuf_tensor("ones_fs", [1, 128], F32))
        io["d_ffn_out"] = {}
        io["eps_t"] = E(nc.sbuf_tensor("eps_t", [128, 1], F32))
        sc = Sched(nc, ctx, "b")
        sc.phase("pre")
        sc.op("vector", lambda q: q.memset(io["eps_t"][:], EPS))
        t1 = sc.dma_in("sync", io["id_s"][:], io["ident"][:])
        t2 = sc.dma_in("sync", io["ones_fs"][:], io["ones_f"][:])
        io["tok_pre"] = [t1, t2]
        sc.emit_ph()
        emit_load_R(sc, io, io["h1"])
        emit_l0_ffn(sc, E, io)
        sc.finalize()
    return nc





def emit_l1_attn(sc: Sched, E, io):
    nc = sc.nc
    R, id_s = io["R"], io["id_s"]
    h3 = io.get("h3")
    inter = io.get("write_inter", True)
    ones_f = io["ones_fs"]

    with ExitStack() as sctx:
        SE = sctx.enter_context
        wA = SE(nc.sbuf_tensor("l1wA", [128, 8, D], BF16))
        wB = SE(nc.sbuf_tensor("l1wB", [128, 8, D], BF16))
        onec = SE(nc.sbuf_tensor("onec", [128, 1], BF16))
        borow = SE(nc.sbuf_tensor("borow", [1, D], F32))
        bo_bc = SE(nc.sbuf_tensor("bo_bc", [128, D], F32))
        ctx_s = SE(nc.sbuf_tensor("ctx_s", [128, 8, D], BF16))
        xTr = SE(nc.sbuf_tensor("xTr", [128, 2, 8, 128], BF16))
        qsum = SE(nc.sbuf_tensor("qsum", [128, 1], F32))
        qrec = SE(nc.sbuf_tensor("qrec", [128, 1], F32))
        zcol = SE(nc.sbuf_tensor("zcol", [128, 8], F32))
        zzs = SE(nc.sbuf_tensor("zzs", [128, 8], F32))
        SF2 = SE(nc.sbuf_tensor("l1SF2", [128, D], F32))
        SF3 = SE(nc.sbuf_tensor("l1SF3", [128, D], F32))
        bn6c = SE(nc.sbuf_tensor("bn6c", [128, 12], F32))
        mvc = SE(nc.sbuf_tensor("mvc", [128, 2], F32))
        vepsc = SE(nc.sbuf_tensor("vepsc", [128, 1], F32))
        sdc = SE(nc.sbuf_tensor("sdc", [128, 1], F32))
        rstdc = SE(nc.sbuf_tensor("rstdc", [128, 1], F32))
        mnrc = SE(nc.sbuf_tensor("mnrc", [128, 1], F32))
        OUTC = SE(nc.sbuf_tensor("OUTC", [128, 1, D], F32))

        d_wk = d_wv = None
        for seq in range(NSEQ):
            Roff = seq * NT
            with ExitStack() as ektx:
                ek = ektx.enter_context(nc.sbuf_tensor(f"ek{seq}", [128, NT, D], BF16))
                vt = ektx.enter_context(nc.sbuf_tensor(f"vt{seq}", [128, NT, D], BF16))

                with ExitStack() as pctx:
                    PE = pctx.enter_context
                    XTP = [PE(nc.psum_tensor(f"XTP{seq}_{i}", [128, D], BF16)) for i in range(2)]
                    KPP = [PE(nc.psum_tensor(f"KPP{seq}_{i}", [128, D], F32)) for i in range(2)]

                    sc.phase(f"l1s{seq}_d1")
                    if seq == 0:
                        d_wk = sc.dma_in("gpsimd", wA[:], io["Wk"].rearrange("(c p) o -> p c o", p=128))
                        d_wv = sc.dma_in("gpsimd", wB[:], io["Wv"].rearrange("(c p) o -> p c o", p=128))
                        d_oc = sc.dma_in("sync", onec[:], io["ones_col"][:])
                        d_bo = sc.dma_in("sync", borow[:], io["bo_row"][:])
                        io["_l1tok"] = {"d_oc": d_oc, "d_bo": d_bo}
                    ev = {}
                    s_k = {}
                    for t in range(NT):
                        par = t % 2
                        for c in range(8):
                            m = sc.tmm(lambda q, t=t, c=c, par=par: q.transpose(
                                XTP[par][:, c * 128:(c + 1) * 128],
                                R[:, Roff + t, c * 128:(c + 1) * 128], id_s[:]),
                                waits=[ev.get(t - 2) if c == 0 else None] +
                                      (io.get("tok_pre", [])
                                       if (seq == 0 and t == 0 and c == 0) else []),
                                mark=(c == 7))
                        ev[t] = sc.op("vector", lambda q, par=par: q.tensor_copy(
                            xTr[:, par, :, :].rearrange("p a b -> p (a b)"), XTP[par][:]),
                            waits=[m])
                        for (cs, wdt) in ((0, 512), (512, 512)):
                            for c in range(8):
                                mk = sc.tmm(lambda q, c=c, cs=cs, par=par: q.matmul(
                                    KPP[par][:, cs:cs + 512], xTr[:, par, c, :],
                                    wA[:, c, cs:cs + 512], start=(c == 0), stop=(c == 7)),
                                    waits=[ev[t], d_wk, s_k.get(t - 2)] if (c == 0 and cs == 0) else [],
                                    mark=(c == 7 and cs == 512))
                        s_k[t] = sc.op("scalar", lambda q, t=t, par=par: q.activation(
                            ek[:, t, :], KPP[par][:], AF.Exp), waits=[mk])
                    sc.emit_ph()

                with ExitStack() as pctx:
                    PE = pctx.enter_context
                    XTP = [PE(nc.psum_tensor(f"XTQ{seq}_{i}", [128, D], BF16)) for i in range(2)]
                    VPP = [PE(nc.psum_tensor(f"VPP{seq}_{i}", [128, D], F32)) for i in range(2)]

                    sc.phase(f"l1s{seq}_d2")
                    ev = {}
                    s_v = {}
                    for t in range(NT):
                        par = t % 2
                        for c in range(8):
                            m = sc.tmm(lambda q, t=t, c=c, par=par: q.transpose(
                                XTP[par][:, c * 128:(c + 1) * 128],
                                R[:, Roff + t, c * 128:(c + 1) * 128], id_s[:]),
                                waits=[ev.get(t - 2) if c == 0 else None],
                                mark=(c == 7))
                        ev[t] = sc.op("vector", lambda q, par=par: q.tensor_copy(
                            xTr[:, par, :, :].rearrange("p a b -> p (a b)"), XTP[par][:]),
                            waits=[m])
                        for (cs, wdt) in ((0, 512), (512, 512)):
                            for c in range(8):
                                mv_ = sc.tmm(lambda q, c=c, cs=cs, par=par: q.matmul(
                                    VPP[par][:, cs:cs + 512], xTr[:, par, c, :],
                                    wB[:, c, cs:cs + 512], start=(c == 0), stop=(c == 7)),
                                    waits=[ev[t], d_wv, s_v.get(t - 2)] if (c == 0 and cs == 0) else [],
                                    mark=(c == 7 and cs == 512))
                        s_v[t] = sc.op("scalar", lambda q, t=t, par=par: q.copy(
                            vt[:, t, :], VPP[par][:]), waits=[mv_])
                    d_wq = sc.dma_in("gpsimd", wA[:], io["Wq"].rearrange("(c p) o -> p c o", p=128),
                                     waits=[mk])
                    sc.emit_ph()

                with ExitStack() as pctx:
                    PE = pctx.enter_context
                    ZP = PE(nc.psum_tensor(f"ZPx{seq}", [1, D], F32))
                    ZTC = PE(nc.psum_tensor(f"ZTCx{seq}", [128, 8], F32))
                    CXP = [PE(nc.psum_tensor(f"CXP{seq}_{i}", [128, 512], F32)) for i in range(2)]

                    sc.phase(f"l1s{seq}_d3")
                    for (cs, wdt) in ((0, 512), (512, 512)):
                        for t in range(NT):
                            mz = sc.tmm(lambda q, t=t, cs=cs: q.matmul(
                                ZP[:, cs:cs + 512], onec[:], ek[:, t, cs:cs + 512],
                                start=(t == 0), stop=(t == NT - 1)),
                                waits=[io["_l1tok"]["d_oc"]] if (t == 0 and cs == 0) else [],
                                mark=(t == NT - 1 and cs == 512))
                    s_zr = sc.op("scalar", lambda q: q.copy(SF2[0:1, :], ZP[:]), waits=[mz])
                    mzt = None
                    for dt in range(8):
                        mzt = sc.tmm(lambda q, dt=dt: q.transpose(
                            ZTC[:, dt:dt + 1], SF2[0:1, dt * 128:(dt + 1) * 128],
                            io["id_f"][0:1, 0:1]), waits=[s_zr if dt == 0 else None],
                            mark=(dt == 7))
                    sc.op("vector", lambda q: q.tensor_copy(zcol[:], ZTC[:]), waits=[mzt])
                    sc.op("vector", lambda q: q.reciprocal(zzs[:], zcol[:]))
                    v_z3 = sc.op("vector", lambda q: q.tensor_scalar(
                        zzs[:], zzs[:], 0.125 / 1024.0, None, ALU.mult))
                    ctx_ev = {}
                    for dt in range(8):
                        for ecs in range(2):
                            k2 = dt * 2 + ecs
                            par = k2 % 2
                            for t in range(NT):
                                mc = sc.tmm(lambda q, dt=dt, ecs=ecs, t=t, par=par: q.matmul(
                                    CXP[par][:], ek[:, t, dt * 128:(dt + 1) * 128],
                                    vt[:, t, ecs * 512:(ecs + 1) * 512],
                                    start=(t == 0), stop=(t == NT - 1)),
                                    waits=[ctx_ev.get(k2 - 2)] if t == 0 else [],
                                    mark=(t == NT - 1))
                            ctx_ev[k2] = sc.op(
                                "vector", lambda q, dt=dt, ecs=ecs, par=par: q.tensor_scalar(
                                    ctx_s[:, dt, ecs * 512:(ecs + 1) * 512], CXP[par][:],
                                    zzs[:, dt:dt + 1], None, ALU.mult),
                                waits=[mc, v_z3 if k2 == 0 else None])
                    d_wo = sc.dma_in("gpsimd", wB[:],
                                     io["Wo"].rearrange("(c p) o -> p c o", p=128),
                                     waits=[mc])
                    sc.emit_ph()

            with ExitStack() as d4ctx:
                qTf = d4ctx.enter_context(nc.sbuf_tensor(f"qTf{seq}", [128, 8, S], BF16))
                pctx = d4ctx.enter_context(ExitStack())
                PE = pctx.enter_context
                qht = PE(nc.sbuf_tensor(f"qht{seq}", [128, 2, D], BF16))
                TPP = [PE(nc.psum_tensor(f"TPa{seq}_{i}", [128, D], BF16)) for i in range(3)]
                QPP = [PE(nc.psum_tensor(f"QPx{seq}_{i}", [128, D], F32)) for i in range(2)]

                sc.phase(f"l1s{seq}_d4a")
                ev_tp = {}
                s_q = {}
                v_rc = {}
                use = [0]

                def tp_group(src_fn, dst_fn, wait0):
                    u = use[0]
                    use[0] += 1
                    slot = u % 3
                    for c in range(8):
                        m = sc.tmm(lambda q, c=c, slot=slot, src_fn=src_fn: q.transpose(
                            TPP[slot][:, c * 128:(c + 1) * 128],
                            src_fn(c), id_s[:]),
                            waits=[wait0, ev_tp.get(u - 3)] if c == 0 else [],
                            mark=(c == 7))
                    ev_tp[u] = sc.op("vector", lambda q, slot=slot, dst_fn=dst_fn: q.tensor_copy(
                        dst_fn(), TPP[slot][:]), waits=[m])
                    return ev_tp[u]

                def emit_qht_tp(t):
                    par = t % 2
                    return tp_group(
                        lambda c, t=t, par=par: qht[:, par, c * 128:(c + 1) * 128],
                        lambda t=t: qTf[:, :, t * 128:(t + 1) * 128],
                        s_q[t])

                for t in range(NT):
                    par = t % 2
                    exr = tp_group(
                        lambda c, t=t: R[:, Roff + t, c * 128:(c + 1) * 128],
                        lambda par=par: xTr[:, par, :, :].rearrange("p a b -> p (a b)"),
                        None)
                    for (cs, wdt) in ((0, 512), (512, 512)):
                        for c in range(8):
                            mq = sc.tmm(lambda q, c=c, cs=cs, par=par: q.matmul(
                                QPP[par][:, cs:cs + 512], xTr[:, par, c, :],
                                wA[:, c, cs:cs + 512], start=(c == 0), stop=(c == 7)),
                                waits=[exr, d_wq, s_q.get(t - 2)]
                                      if (c == 0 and cs == 0) else [],
                                mark=(c == 7 and cs == 512))
                    sq = sc.op("scalar", lambda q, par=par: q.activation(
                        qht[:, par, :], QPP[par][:], AF.Exp, accum_out=qsum[:]),
                        waits=[mq, v_rc.get(t - 1)])
                    v_rc[t] = sc.op("vector", lambda q: q.reciprocal(qrec[:], qsum[:]),
                                    waits=[sq])
                    s_q[t] = sc.op("vector", lambda q, par=par: q.tensor_scalar(
                        qht[:, par, :], qht[:, par, :], qrec[:], None, ALU.mult))
                    if t >= 1:
                        emit_qht_tp(t - 1)
                emit_qht_tp(NT - 1)
                sc.emit_ph()
                pctx.close()

                with ExitStack() as qctx:
                    QE = qctx.enter_context
                    ytk = QE(nc.sbuf_tensor(f"ytk{seq}", [128, 2, D], BF16))
                    yTr = QE(nc.sbuf_tensor(f"yTr{seq}", [128, 2, 8, 128], BF16))
                    TPB = [QE(nc.psum_tensor(f"TPb{seq}_{i}", [128, D], BF16)) for i in range(2)]
                    YP = [QE(nc.psum_tensor(f"YPx{seq}_{i}", [128, D], F32)) for i in range(2)]
                    OP = QE(nc.psum_tensor(f"OPx{seq}", [128, D], F32))

                    sc.phase(f"l1s{seq}_d4b")
                    if seq == 0:
                        for (cs, wdt) in ((0, 512), (512, 512)):
                            mb = sc.tmm(lambda q, cs=cs: q.matmul(
                                OP[:, cs:cs + 512], ones_f[:], borow[:, cs:cs + 512],
                                start=True, stop=True),
                                waits=[io["_l1tok"]["d_bo"]] if cs == 0 else [],
                                mark=(cs == 512))
                        v_bo = sc.op("vector", lambda q: q.tensor_copy(bo_bc[:], OP[:]),
                                     waits=[mb])
                    yguard = {}
                    s_o = {}
                    v_y = {}

                    def emit_tail(t):
                        gt = Roff + t
                        par = t % 2
                        for c in range(8):
                            m3 = sc.tmm(lambda q, c=c, par=par: q.transpose(
                                TPB[par][:, c * 128:(c + 1) * 128],
                                ytk[:, par, c * 128:(c + 1) * 128], id_s[:]),
                                waits=[v_y[t], yguard.get(t - 2),
                                       v_bo if (seq == 0 and t == 0) else None]
                                      if c == 0 else [],
                                mark=(c == 7))
                        yguard[t] = sc.op("vector", lambda q, par=par: q.tensor_copy(
                            yTr[:, par, :, :].rearrange("p a b -> p (a b)"), TPB[par][:]),
                            waits=[m3])
                        for (cs, wdt) in ((0, 512), (512, 512)):
                            for c in range(8):
                                mo = sc.tmm(lambda q, c=c, cs=cs, par=par: q.matmul(
                                    OP[:, cs:cs + 512], yTr[:, par, c, :],
                                    wB[:, c, cs:cs + 512], start=(c == 0), stop=(c == 7)),
                                    waits=[yguard[t], d_wo] if (c == 0 and cs == 0) else [],
                                    mark=(c == 7 and cs == 512))
                        return gt, mo

                    def emit_ln(gt, mo, t):
                        v_sf2 = sc.op("vector", lambda q: q.tensor_tensor(
                            SF2[:], OP[:], bo_bc[:], ALU.add), waits=[mo])
                        g_sf3 = sc.op("gpsimd", lambda q, gt=gt: q.tensor_tensor(
                            SF3[:], SF2[:], R[:, gt, :], ALU.add), waits=[v_sf2])
                        sc.op("vector", lambda q: q.bn_stats(bn6c[:, 0:6], SF3[:, 0:512]),
                              waits=[g_sf3])
                        sc.op("vector", lambda q: q.bn_stats(bn6c[:, 6:12], SF3[:, 512:1024]))
                        v_eps = sc.op("vector", lambda q: q.bn_aggr(mvc[:], bn6c[:]))
                        s_ln = sc.op("scalar", lambda q: q.activation(
                            sdc[:], mvc[:, 1:2], AF.Ln, bias=io["eps_t"][:]), waits=[v_eps])
                        s_sd = sc.op("scalar", lambda q: q.activation(
                            rstdc[:], sdc[:], AF.Exp, scale=-0.5))
                        v_mnr = sc.op("vector", lambda q: q.tensor_scalar(
                            mnrc[:], mvc[:, 0:1], rstdc[:], -1.0, ALU.mult, ALU.mult),
                            waits=[s_sd])
                        if inter:
                            sc.op("vector", lambda q: q.tensor_scalar(
                                OUTC[:, 0, :], SF3[:], mvc[:, 0:1], rstdc[:],
                                ALU.subtract, ALU.mult),
                                waits=[s_sd, io["d_l1_out"].get(gt - 1)])
                            s_o[t] = sc.op("vector", lambda q, gt=gt: q.tensor_copy(
                                R[:, gt, :], OUTC[:, 0, :]))
                            io["d_l1_out"][gt] = sc.dma_out(
                                "sync", h3[seq, t * 128:(t + 1) * 128, :], OUTC[:, 0, :],
                                waits=[s_o[t]])
                        else:
                            s_o[t] = sc.op("gpsimd", lambda q, gt=gt: q.tensor_scalar(
                                R[:, gt, :], SF3[:], rstdc[:], mnrc[:],
                                ALU.mult, ALU.add),
                                waits=[v_mnr])

                    pending = None
                    for t in range(NT):
                        par = t % 2
                        for (cs, wdt) in ((0, 512), (512, 512)):
                            for c in range(8):
                                my = sc.tmm(lambda q, t=t, c=c, cs=cs, par=par: q.matmul(
                                    YP[par][:, cs:cs + 512], qTf[:, c, t * 128:(t + 1) * 128],
                                    ctx_s[:, c, cs:cs + 512], start=(c == 0), stop=(c == 7)),
                                    waits=[v_y.get(t - 2)] if (c == 0 and cs == 0) else [],
                                    mark=(c == 7 and cs == 512))
                        v_y[t] = sc.op("vector", lambda q, par=par: q.tensor_copy(
                            ytk[:, par, :], YP[par][:]), waits=[my])
                        if pending is not None:
                            gtp, mop = emit_tail(pending)
                            emit_ln(gtp, mop, pending)
                        pending = t
                    gtp, mop = emit_tail(pending)
                    emit_ln(gtp, mop, pending)
                if seq == 0:
                    d_wk = sc.dma_in("gpsimd", wA[:],
                                     io["Wk"].rearrange("(c p) o -> p c o", p=128),
                                     waits=[mop])
                    d_wv = sc.dma_in("gpsimd", wB[:],
                                     io["Wv"].rearrange("(c p) o -> p c o", p=128),
                                     waits=[mop])
                sc.drain_outs()
                sc.emit_ph()


def build_launch_c():
    nc = bass.Bass("TRN2", target_bir_lowering=False)
    io = {
        "h2in": nc.dram_tensor("h2in", [NSEQ, S, D], F32, kind="ExternalInput").ap(),
        "Wq": nc.dram_tensor("Wq", [D, D], F32, kind="ExternalInput").ap(),
        "Wk": nc.dram_tensor("Wk", [D, D], F32, kind="ExternalInput").ap(),
        "Wv": nc.dram_tensor("Wv", [D, D], F32, kind="ExternalInput").ap(),
        "Wo": nc.dram_tensor("Wo", [D, D], F32, kind="ExternalInput").ap(),
        "bo_row": nc.dram_tensor("bo_row", [1, D], F32, kind="ExternalInput").ap(),
        "ones_col": nc.dram_tensor("ones_col", [128, 1], BF16, kind="ExternalInput").ap(),
        "ones_f": nc.dram_tensor("ones_f", [1, 128], F32, kind="ExternalInput").ap(),
        "ident": nc.dram_tensor("ident", [128, 128], BF16, kind="ExternalInput").ap(),
        "identf": nc.dram_tensor("identf", [128, 128], F32, kind="ExternalInput").ap(),
        "h3": nc.dram_tensor("h3", [NSEQ, S, D], F32, kind="ExternalOutput").ap(),
    }
    with ExitStack() as ctx:
        E = ctx.enter_context
        io["R"] = E(nc.sbuf_tensor("R", [128, NSEQ * NT, D], BF16))
        io["id_s"] = E(nc.sbuf_tensor("id_s", [128, 128], BF16))
        io["id_f"] = E(nc.sbuf_tensor("id_f", [128, 128], F32))
        io["ones_fs"] = E(nc.sbuf_tensor("ones_fs", [1, 128], F32))
        io["d_l1_out"] = {}
        io["eps_t"] = E(nc.sbuf_tensor("eps_t", [128, 1], F32))
        sc = Sched(nc, ctx, "c")
        sc.phase("pre")
        sc.op("vector", lambda q: q.memset(io["eps_t"][:], EPS))
        t1 = sc.dma_in("sync", io["id_s"][:], io["ident"][:])
        t2 = sc.dma_in("sync", io["id_f"][:], io["identf"][:])
        t3 = sc.dma_in("sync", io["ones_fs"][:], io["ones_f"][:])
        io["tok_pre"] = [t1, t2, t3]
        sc.emit_ph()
        emit_load_R(sc, io, io["h2in"])
        emit_l1_attn(sc, E, io)
        sc.finalize()
    return nc


def host_l1_attn_inputs(inputs):
    return {
        "Wq": np.ascontiguousarray(np.asarray(inputs["L1_attn_Wq"], np.float32)),
        "Wk": np.ascontiguousarray(np.asarray(inputs["L1_attn_Wk"], np.float32)),
        "Wv": np.ascontiguousarray(np.asarray(inputs["L1_attn_Wv"], np.float32)),
        "Wo": np.ascontiguousarray(np.asarray(inputs["L1_attn_Wo"], np.float32)),
        "bo_row": np.ascontiguousarray(np.asarray(inputs["L1_attn_bo"], np.float32).reshape(1, D)),
        "ones_col": np.ones((128, 1), ml_dtypes.bfloat16),
        "ones_f": np.ones((1, 128), np.float32),
        "ident": np.eye(128, dtype=ml_dtypes.bfloat16),
        "identf": np.eye(128, dtype=np.float32),
    }




def emit_l1_moe(sc: Sched, E, io):
    nc = sc.nc
    R, id_s = io["R"], io["id_s"]
    out = io["out"]
    NTT = NSEQ * NT

    bctx = ExitStack()
    E = bctx.enter_context
    comb = E(nc.sbuf_tensor("comb", [128, NTT, 8], F32))
    routT = E(nc.sbuf_tensor("routT", [128, 8, 8], BF16))

    GUT = nc.dram_tensor("GUT", [8, 16, 128, 8, 128], BF16, kind="Internal").ap()
    DPT = nc.dram_tensor("DPT", [8, 128, 8, D], BF16, kind="Internal").ap()

    with ExitStack() as pctx:
        PE = pctx.enter_context
        gub = PE(nc.sbuf_tensor("gub", [128, 2, 16, D], BF16))
        dpb = PE(nc.sbuf_tensor("dpb", [128, 2, 8, D], BF16))
        WSTD = [PE(nc.sbuf_tensor(f"WSTD{i}", [128, 8, 128], BF16)) for i in range(2)]
        PTD = [PE(nc.psum_tensor(f"PTD{i}", [128, D], BF16)) for i in range(2)]

        d_gu = {}
        d_dp = {}
        sc.phase("moe_m0")
        d_gu[0] = sc.dma_in("gpsimd", gub[:, 0, :, :],
                            io["gate_up"][0].rearrange("(t p) h -> p t h", p=128))
        d_dp[0] = sc.dma_in("gpsimd", dpb[:, 0, :, :],
                            io["down_p"][0].rearrange("(c p) f -> p c f", p=128))
        dW = {}
        m_last = {}
        for e in range(8):
            sl = e % 2
            if e + 1 < 8:
                d_gu[e + 1] = sc.dma_in("gpsimd", gub[:, 1 - sl, :, :],
                                        io["gate_up"][e + 1].rearrange("(t p) h -> p t h", p=128),
                                        waits=[m_last.get(e - 1)])
                d_dp[e + 1] = sc.dma_in("gpsimd", dpb[:, 1 - sl, :, :],
                                        io["down_p"][e + 1].rearrange("(c p) f -> p c f", p=128),
                                        waits=[None])
            for ft in range(16):
                par = ft % 2
                for c in range(8):
                    m = sc.tmm(lambda q, ft=ft, c=c, par=par, sl=sl: q.transpose(
                        PTD[par][:, c * 128:(c + 1) * 128],
                        gub[:, sl, ft, c * 128:(c + 1) * 128], id_s[:]),
                        waits=[d_gu[e] if (ft == 0 and c == 0) else None,
                               dW.get((e, ft - 2)) if c == 0 else None] +
                              (io.get("tok_pre", [])
                               if (e == 0 and ft == 0 and c == 0) else []),
                        mark=(c == 7))
                vv = sc.op("vector", lambda q, par=par: q.tensor_copy(
                    WSTD[par][:].rearrange("p c f -> p (c f)"), PTD[par][:]),
                    waits=[m, dW.get((e, ft - 2))])
                dW[(e, ft)] = sc.dma_out("sync", GUT[e, ft], WSTD[par][:], waits=[vv])
            for fc in range(8):
                par = fc % 2
                for hc in range(8):
                    m = sc.tmm(lambda q, fc=fc, hc=hc, par=par, sl=sl: q.transpose(
                        PTD[par][:, hc * 128:(hc + 1) * 128],
                        dpb[:, sl, hc, fc * 128:(fc + 1) * 128], id_s[:]),
                        waits=[d_dp[e] if (fc == 0 and hc == 0) else None,
                               dW.get((e, 16 + fc - 2)) if hc == 0 else None],
                        mark=(hc == 7))
                vv = sc.op("vector", lambda q, par=par: q.tensor_copy(
                    WSTD[par][:].rearrange("p c f -> p (c f)"), PTD[par][:]),
                    waits=[m, dW.get((e, 16 + fc - 2))])
                dW[(e, 16 + fc)] = sc.dma_out("sync", DPT[e][:, fc, :], WSTD[par][:],
                                              waits=[vv])
            m_last[e] = m
            dW[(e + 1, -2)] = dW[(e, 22)]
            dW[(e + 1, -1)] = dW[(e, 23)]
        sc.drain_outs()
        sc.emit_ph()

    with ExitStack() as pctx:
        PE = pctx.enter_context
        rw_s = PE(nc.sbuf_tensor("rw_s", [8, D], BF16))
        xTr = PE(nc.sbuf_tensor("xTrm", [128, 2, 8, 128], BF16))
        m1 = PE(nc.sbuf_tensor("m1", [128, 1], F32))
        m2 = PE(nc.sbuf_tensor("m2", [128, 1], F32))
        d12 = PE(nc.sbuf_tensor("d12", [128, 1], F32))
        w1t = PE(nc.sbuf_tensor("w1t", [128, 1], F32))
        w2t = PE(nc.sbuf_tensor("w2t", [128, 1], F32))
        eq1 = PE(nc.sbuf_tensor("eq1", [128, 8], F32))
        eq2 = PE(nc.sbuf_tensor("eq2", [128, 8], F32))
        v2b = PE(nc.sbuf_tensor("v2b", [128, 8], F32))
        tmp8 = PE(nc.sbuf_tensor("tmp8", [128, 8], F32))
        PXT = [PE(nc.psum_tensor(f"PXTm{i}", [128, D], BF16)) for i in range(2)]
        RTP = PE(nc.psum_tensor("RTP", [128, 8], BF16))
        RLP = [PE(nc.psum_tensor(f"RLP{i}", [128, 8], F32)) for i in range(2)]

        sc.phase("moe_m1")
        d_rw = sc.dma_in("gpsimd", rw_s[:], io["router_w"][:])
        mrt = None
        s_rt = None
        for c in range(8):
            mrt = sc.tmm(lambda q, c=c: q.transpose(
                RTP[:, 0:8], rw_s[:, c * 128:(c + 1) * 128], id_s[0:8, 0:8]),
                waits=[d_rw if c == 0 else None,
                       (sc.sem["scalar"], sc.sem["scalar"].val) if c > 0 else None],
                mark=True)
            s_rt = sc.op("scalar", lambda q, c=c: q.copy(routT[:, c, :], RTP[:, 0:8]),
                         waits=[mrt])
        ev = {}
        s_cb = {}
        for gt in range(NTT):
            par = gt % 2
            for c in range(8):
                m = sc.tmm(lambda q, gt=gt, c=c, par=par: q.transpose(
                    PXT[par][:, c * 128:(c + 1) * 128],
                    R[:, gt, c * 128:(c + 1) * 128], id_s[:]),
                    waits=[ev.get(gt - 2) if c == 0 else None],
                    mark=(c == 7))
            ev[gt] = sc.op("vector", lambda q, par=par: q.tensor_copy(
                xTr[:, par, :, :].rearrange("p a b -> p (a b)"), PXT[par][:]),
                waits=[m])
            for c in range(8):
                ml = sc.tmm(lambda q, gt=gt, c=c, par=par: q.matmul(
                    RLP[par][:], xTr[:, par, c, :], routT[:, c, :],
                    start=(c == 0), stop=(c == 7)),
                    waits=[ev[gt], s_cb.get(gt - 2),
                           s_rt if gt == 0 else None] if c == 0 else [],
                    mark=(c == 7))
            sc.op("vector", lambda q, par=par: q.reduce_max(m1[:], RLP[par][:], axis=AX.X),
                  waits=[ml])
            sc.op("vector", lambda q, par=par: q.tensor_scalar(
                eq1[:], RLP[par][:], m1[:], None, ALU.is_equal))
            sc.op("vector", lambda q, par=par: q.scalar_tensor_tensor(
                v2b[:], eq1[:], -1e30, RLP[par][:], ALU.mult, ALU.add))
            sc.op("vector", lambda q: q.reduce_max(m2[:], v2b[:], axis=AX.X))
            v_d = sc.op("vector", lambda q: q.tensor_tensor(d12[:], m1[:], m2[:], ALU.subtract))
            s_w1 = sc.op("scalar", lambda q: q.activation(w1t[:], d12[:], AF.Sigmoid),
                         waits=[v_d])
            sc.op("vector", lambda q: q.tensor_scalar(
                w2t[:], w1t[:], -1.0, 1.0, ALU.mult, ALU.add), waits=[s_w1])
            sc.op("vector", lambda q, par=par: q.tensor_scalar(
                eq2[:], RLP[par][:], m2[:], None, ALU.is_equal))
            sc.op("vector", lambda q: q.tensor_scalar(
                tmp8[:], eq1[:], w1t[:], None, ALU.mult))
            s_cb[gt] = sc.op("vector", lambda q, gt=gt: q.scalar_tensor_tensor(
                comb[:, gt, :], eq2[:], w2t[:], tmp8[:], ALU.mult, ALU.add))
        sc.emit_ph()

    with ExitStack() as pctx:
        PE = pctx.enter_context
        xTc = PE(nc.sbuf_tensor("xTcm", [128, 8, 512], BF16))
        gur = [PE(nc.sbuf_tensor(f"gur{i}", [128, 8, 128], BF16)) for i in range(4)]
        dpr = [PE(nc.sbuf_tensor(f"dpr{i}", [128, 8, D], BF16)) for i in range(2)]
        zbuf = PE(nc.sbuf_tensor("zbuf", [128, 8, 512], BF16))
        zg = PE(nc.sbuf_tensor("zg", [128, 512], BF16))
        accA = PE(nc.sbuf_tensor("accA", [128, 4, D], F32))
        accB = PE(nc.sbuf_tensor("accB", [128, 4, D], F32))
        SF3 = PE(nc.sbuf_tensor("mSF3", [128, D], F32))
        bn6d = PE(nc.sbuf_tensor("bn6d", [128, 12], F32))
        mvd = PE(nc.sbuf_tensor("mvd", [128, 2], F32))
        vepsd = PE(nc.sbuf_tensor("vepsd", [128, 1], F32))
        sdd = PE(nc.sbuf_tensor("sdd", [128, 1], F32))
        rstdd = PE(nc.sbuf_tensor("rstdd", [128, 1], F32))
        OUTD = PE(nc.sbuf_tensor("OUTD", [128, 2, D], F32))

        NCHUNK = NTT // 4
        with ExitStack() as ectx:
            PXT = ectx.enter_context(nc.psum_tensor("PXTz", [128, D], BF16))
            PG = [ectx.enter_context(nc.psum_tensor(f"PGz{i}", [128, 512], F32))
                  for i in range(3)]
            PD = [ectx.enter_context(nc.psum_tensor(f"PDz{i}", [128, D], F32))
                  for i in range(2)]
            pg_tok = {}
            prev_chunk_md = None
            for cc in range(NCHUNK):
                sc.phase(f"moe_m2_{cc}")
                gt0 = cc * 4
                evx = {}
                for j in range(4):
                    for c in range(8):
                        m = sc.tmm(lambda q, j=j, c=c: q.transpose(
                            PXT[:, c * 128:(c + 1) * 128],
                            R[:, gt0 + j, c * 128:(c + 1) * 128], id_s[:]),
                            waits=[evx.get(j - 1)] if c == 0 else [],
                            mark=(c == 7))
                    evx[j] = sc.op("vector", lambda q, j=j: q.tensor_copy(
                        xTc[:, :, j * 128:(j + 1) * 128],
                        PXT[:].rearrange("p (c f) -> p c f", c=8)), waits=[m])

                d_g = {}
                d_d = {}
                acc_toks = {}
                mgk = {}
                prev_md = None
                prev_acc_tok = None
                prev_vz = None
                d_g[(0, 0)] = sc.dma_in("sync", gur[0][:], GUT[0, 0])
                d_g[(0, 1)] = sc.dma_in("sync", gur[1][:], GUT[0, 8])
                d_g[(0, 2)] = sc.dma_in("sync", gur[2][:], GUT[0, 1])
                d_g[(0, 3)] = sc.dma_in("sync", gur[3][:], GUT[0, 9])
                d_d[0] = sc.dma_in("gpsimd", dpr[0][:], DPT[0])
                for e in range(8):
                    v_z = {}
                    s_zg = {}
                    for fi in range(8):
                        for half, ft in ((0, fi), (1, fi + 8)):
                            k = 2 * fi + half
                            slot = k % 4
                            gslot = k % 3
                            if (e, k) not in d_g:
                                d_g[(e, k)] = sc.dma_in(
                                    "sync", gur[slot][:], GUT[e, ft],
                                    waits=[mgk.get((e, k - 4)) if k >= 4
                                           else mgk.get((e - 1, k + 12))])
                            for c in range(8):
                                mg = sc.tmm(lambda q, c=c, slot=slot, gslot=gslot: q.matmul(
                                    PG[gslot][:], gur[slot][:, c, :], xTc[:, c, :],
                                    start=(c == 0), stop=(c == 7)),
                                    waits=[d_g[(e, k)], evx[3], pg_tok.get((cc, e, k - 3)),
                                           prev_md if (fi == 0 and half == 0) else None,
                                           prev_chunk_md if (e == 0 and k <= 1) else None]
                                          if c == 0 else [],
                                    mark=(c == 7))
                            mgk[(e, k)] = mg
                        s_zg[fi] = sc.op("scalar", lambda q, fi=fi: q.activation(
                            zg[:], PG[(2 * fi) % 3][:], AF.Silu),
                            waits=[mgk[(e, 2 * fi + 1)], v_z.get(fi - 1),
                                   prev_vz if fi == 0 else None])
                        pg_tok[(cc, e, 2 * fi)] = s_zg[fi]
                        v_z[fi] = sc.op("vector", lambda q, fi=fi: q.tensor_tensor(
                            zbuf[:, fi, :], zg[:], PG[(2 * fi + 1) % 3][:], ALU.mult),
                            waits=[s_zg[fi], prev_md if fi == 0 else None])
                        pg_tok[(cc, e, 2 * fi + 1)] = v_z[fi]
                    if e + 1 < 8:
                        d_g[(e + 1, 0)] = sc.dma_in("sync", gur[0][:], GUT[e + 1, 0],
                                                    waits=[mgk[(e, 12)]])
                        d_g[(e + 1, 1)] = sc.dma_in("sync", gur[1][:], GUT[e + 1, 8],
                                                    waits=[mgk[(e, 13)]])
                        d_g[(e + 1, 2)] = sc.dma_in("sync", gur[2][:], GUT[e + 1, 1],
                                                    waits=[mgk[(e, 14)]])
                        d_g[(e + 1, 3)] = sc.dma_in("sync", gur[3][:], GUT[e + 1, 9],
                                                    waits=[mgk[(e, 15)]])
                        d_d[e + 1] = sc.dma_in("gpsimd", dpr[(e + 1) % 2][:], DPT[e + 1],
                                               waits=[prev_acc_tok])
                        pg_tok[(cc, e + 1, -3)] = pg_tok[(cc, e, 13)]
                        pg_tok[(cc, e + 1, -2)] = pg_tok[(cc, e, 14)]
                        pg_tok[(cc, e + 1, -1)] = pg_tok[(cc, e, 15)]
                    acc_src = accA if e % 2 == 1 else accB
                    acc_dst = accB if e % 2 == 1 else accA
                    for tsub in range(4):
                        ppar = tsub % 2
                        pd_guard = acc_toks.get((e, tsub - 2)) or acc_toks.get((e - 1, tsub + 2))
                        for cs in range(2):
                            for fc in range(8):
                                md = sc.tmm(lambda q, tsub=tsub, cs=cs, fc=fc, ppar=ppar, e=e: q.matmul(
                                    PD[ppar][:, cs * 512:(cs + 1) * 512],
                                    zbuf[:, fc, tsub * 128:(tsub + 1) * 128],
                                    dpr[e % 2][:, fc, cs * 512:(cs + 1) * 512],
                                    start=(fc == 0), stop=(fc == 7)),
                                    waits=([v_z[7], d_d[e], pd_guard]
                                           if (fc == 0 and cs == 0 and tsub == 0) else
                                           ([pd_guard] if (fc == 0 and cs == 0) else [])),
                                    mark=(fc == 7 and cs == 1))
                        gt = gt0 + tsub
                        if e == 0:
                            tok = sc.op("vector", lambda q, tsub=tsub, ppar=ppar, gt=gt: q.tensor_scalar(
                                accA[:, tsub, :], PD[ppar][:], comb[:, gt, 0:1], None, ALU.mult),
                                waits=[md])
                        else:
                            tok = sc.op("vector", lambda q, tsub=tsub, ppar=ppar, gt=gt, e=e,
                                        a=acc_src, b=acc_dst: q.scalar_tensor_tensor(
                                b[:, tsub, :], PD[ppar][:], comb[:, gt, e:e + 1],
                                a[:, tsub, :], ALU.mult, ALU.add),
                                waits=[md])
                        acc_toks[(e, tsub)] = tok
                    prev_acc_tok = tok
                    prev_md = md
                    prev_vz = v_z[7]
                prev_chunk_md = md

                accF = accB
                for tsub in range(4):
                    gt = gt0 + tsub
                    seq, ti = gt // NT, gt % NT
                    sc.op("vector", lambda q, tsub=tsub, gt=gt: q.tensor_tensor(
                        SF3[:], accF[:, tsub, :], R[:, gt, :], ALU.add),
                        waits=[prev_acc_tok])
                    sc.op("vector", lambda q: q.bn_stats(bn6d[:, 0:6], SF3[:, 0:512]))
                    sc.op("vector", lambda q: q.bn_stats(bn6d[:, 6:12], SF3[:, 512:1024]))
                    v_eps = sc.op("vector", lambda q: q.bn_aggr(mvd[:], bn6d[:]))
                    s_sd = sc.op("scalar", lambda q: q.activation(
                        sdd[:], mvd[:, 1:2], AF.Sqrt, bias=io["eps_t"][:]), waits=[v_eps])
                    sc.op("vector", lambda q: q.reciprocal(rstdd[:], sdd[:]), waits=[s_sd])
                    opar = gt % 2
                    sc.op("vector", lambda q, opar=opar: q.tensor_scalar(
                        OUTD[:, opar, :], SF3[:], mvd[:, 0:1], rstdd[:],
                        ALU.subtract, ALU.mult),
                        waits=[io["d_moe_out"].get(gt - 2)])
                    io["d_moe_out"][gt] = sc.dma_out(
                        "sync", out[seq, ti * 128:(ti + 1) * 128, :], OUTD[:, opar, :],
                        waits=[(sc.sem["vector"], sc.sem["vector"].val)])
                sc.drain_outs()
                sc.emit_ph()
    bctx.close()


def build_launch_d():
    nc = bass.Bass("TRN2", target_bir_lowering=False)
    io = {
        "h3in": nc.dram_tensor("h3in", [NSEQ, S, D], F32, kind="ExternalInput").ap(),
        "router_w": nc.dram_tensor("router_w", [8, D], F32, kind="ExternalInput").ap(),
        "gate_up": nc.dram_tensor("gate_up", [8, 2048, D], F32, kind="ExternalInput").ap(),
        "down_p": nc.dram_tensor("down_p", [8, D, D], F32, kind="ExternalInput").ap(),
        "ident": nc.dram_tensor("ident", [128, 128], BF16, kind="ExternalInput").ap(),
        "out": nc.dram_tensor("out", [NSEQ, S, D], F32, kind="ExternalOutput").ap(),
    }
    with ExitStack() as ctx:
        E = ctx.enter_context
        io["R"] = E(nc.sbuf_tensor("R", [128, NSEQ * NT, D], BF16))
        io["id_s"] = E(nc.sbuf_tensor("id_s", [128, 128], BF16))
        io["d_moe_out"] = {}
        io["eps_t"] = E(nc.sbuf_tensor("eps_t", [128, 1], F32))
        sc = Sched(nc, ctx, "d")
        sc.phase("pre")
        sc.op("vector", lambda q: q.memset(io["eps_t"][:], EPS))
        t1 = sc.dma_in("sync", io["id_s"][:], io["ident"][:])
        io["tok_pre"] = [t1]
        sc.emit_ph()
        emit_load_R(sc, io, io["h3in"])
        emit_l1_moe(sc, E, io)
        sc.finalize()
    return nc


def host_l1_moe_inputs(inputs):
    return {
        "router_w": np.ascontiguousarray(np.asarray(inputs["L1_ffn_router_weight"], np.float32)),
        "gate_up": np.ascontiguousarray(np.asarray(inputs["L1_ffn_gate_up_proj"], np.float32)),
        "down_p": np.ascontiguousarray(np.asarray(inputs["L1_ffn_down_proj"], np.float32)),
        "ident": np.eye(128, dtype=ml_dtypes.bfloat16),
    }


def host_l0_attn_inputs(xc, inputs):
    maskq = np.where(np.arange(128)[:, None] >= np.arange(128)[None, :], 0.0,
                     -40000.0).astype(np.float32)
    return {
        "x": np.ascontiguousarray(xc),
        "wq": inputs["L0_attn_w_q"],
        "wk": inputs["L0_attn_w_k"],
        "bq": np.ascontiguousarray(np.asarray(inputs["L0_attn_b_q"]).reshape(QK, 1)),
        "bk": np.ascontiguousarray(np.asarray(inputs["L0_attn_b_k"]).reshape(QK, 1)),
        "maskq": maskq,
        "maskT": np.ascontiguousarray(maskq.T),
        "ident": np.eye(128, dtype=ml_dtypes.bfloat16),
        "ones1": np.ones((1, 128), dtype=ml_dtypes.bfloat16),
    }


def build_launch_a():
    nc = bass.Bass("TRN2", target_bir_lowering=False)
    io = {
        "x": nc.dram_tensor("x", [NSEQ, S, D], F32, kind="ExternalInput").ap(),
        "wq": nc.dram_tensor("wq", [QK, D], F32, kind="ExternalInput").ap(),
        "wk": nc.dram_tensor("wk", [QK, D], F32, kind="ExternalInput").ap(),
        "bq": nc.dram_tensor("bq", [QK, 1], F32, kind="ExternalInput").ap(),
        "bk": nc.dram_tensor("bk", [QK, 1], F32, kind="ExternalInput").ap(),
        "maskq": nc.dram_tensor("maskq", [128, 128], F32, kind="ExternalInput").ap(),
        "maskT": nc.dram_tensor("maskT", [128, 128], F32, kind="ExternalInput").ap(),
        "ident": nc.dram_tensor("ident", [128, 128], BF16, kind="ExternalInput").ap(),
        "ones1": nc.dram_tensor("ones1", [1, 128], BF16, kind="ExternalInput").ap(),
        "h1": nc.dram_tensor("h1", [NSEQ, S, D], F32, kind="ExternalOutput").ap(),
    }
    with ExitStack() as ctx:
        E = ctx.enter_context
        io["R"] = E(nc.sbuf_tensor("R", [128, NSEQ * NT, D], BF16))
        io["id_s"] = E(nc.sbuf_tensor("id_s", [128, 128], BF16))
        io["eps_t"] = E(nc.sbuf_tensor("eps_t", [128, 1], F32))
        sc = Sched(nc, ctx, "a")
        sc.phase("pre")
        sc.op("vector", lambda q: q.memset(io["eps_t"][:], EPS))
        sc.emit_ph()
        emit_l0_attn(sc, E, io)
        sc.finalize()
    return nc



def build_fused():
    nc = bass.Bass("TRN2", target_bir_lowering=False)
    io = {
        "x": nc.dram_tensor("x", [NSEQ, S, D], F32, kind="ExternalInput").ap(),
        "wq": nc.dram_tensor("wq", [QK, D], F32, kind="ExternalInput").ap(),
        "wk": nc.dram_tensor("wk", [QK, D], F32, kind="ExternalInput").ap(),
        "bq": nc.dram_tensor("bq", [QK, 1], F32, kind="ExternalInput").ap(),
        "bk": nc.dram_tensor("bk", [QK, 1], F32, kind="ExternalInput").ap(),
        "maskq": nc.dram_tensor("maskq", [128, 128], F32, kind="ExternalInput").ap(),
        "maskT": nc.dram_tensor("maskT", [128, 128], F32, kind="ExternalInput").ap(),
        "ident": nc.dram_tensor("ident", [128, 128], BF16, kind="ExternalInput").ap(),
        "identf": nc.dram_tensor("identf", [128, 128], F32, kind="ExternalInput").ap(),
        "ones1": nc.dram_tensor("ones1", [1, 128], BF16, kind="ExternalInput").ap(),
        "ones_f": nc.dram_tensor("ones_f", [1, 128], F32, kind="ExternalInput").ap(),
        "ones_col": nc.dram_tensor("ones_col", [128, 1], BF16, kind="ExternalInput").ap(),
        "W_exp": nc.dram_tensor("W_exp", [D, 4096], F32, kind="ExternalInput").ap(),
        "W_con": nc.dram_tensor("W_con", [4096, D], F32, kind="ExternalInput").ap(),
        "b_exp_pm": nc.dram_tensor("b_exp_pm", [128, 32], F32, kind="ExternalInput").ap(),
        "up_bias_pm": nc.dram_tensor("up_bias_pm", [128, 32], F32, kind="ExternalInput").ap(),
        "bfv_up": nc.dram_tensor("bfv_up", [128, 5, 32, 2], F32, kind="ExternalInput").ap(),
        "bfvb_up": nc.dram_tensor("bfvb_up", [128, 5, 4, 16], F32, kind="ExternalInput").ap(),
        "bfv_downT": nc.dram_tensor("bfv_downT", [128, 5, 32, 2], F32, kind="ExternalInput").ap(),
        "SUP": nc.dram_tensor("SUP", [7, 32, 128, 128], F32, kind="ExternalInput").ap(),
        "SUPN": nc.dram_tensor("SUPN", [7, 32, 128, 128], F32, kind="ExternalInput").ap(),
        "SDOWN": nc.dram_tensor("SDOWN", [7, 32, 128, 128], F32, kind="ExternalInput").ap(),
        "SDOWNT": nc.dram_tensor("SDOWNT", [7, 32, 128, 128], F32, kind="ExternalInput").ap(),
        "bcon_row": nc.dram_tensor("bcon_row", [1, D], F32, kind="ExternalInput").ap(),
        "dbias_row": nc.dram_tensor("dbias_row", [1, D], F32, kind="ExternalInput").ap(),
        "Wq": nc.dram_tensor("Wq", [D, D], F32, kind="ExternalInput").ap(),
        "Wk": nc.dram_tensor("Wk", [D, D], F32, kind="ExternalInput").ap(),
        "Wv": nc.dram_tensor("Wv", [D, D], F32, kind="ExternalInput").ap(),
        "Wo": nc.dram_tensor("Wo", [D, D], F32, kind="ExternalInput").ap(),
        "bo_row": nc.dram_tensor("bo_row", [1, D], F32, kind="ExternalInput").ap(),
        "router_w": nc.dram_tensor("router_w", [8, D], F32, kind="ExternalInput").ap(),
        "gate_up": nc.dram_tensor("gate_up", [8, 2048, D], F32, kind="ExternalInput").ap(),
        "down_p": nc.dram_tensor("down_p", [8, D, D], F32, kind="ExternalInput").ap(),
        "out": nc.dram_tensor("out", [NSEQ, S, D], F32, kind="ExternalOutput").ap(),
    }
    with ExitStack() as ctx:
        E = ctx.enter_context
        io["R"] = E(nc.sbuf_tensor("R", [128, NSEQ * NT, D], BF16))
        io["id_s"] = E(nc.sbuf_tensor("id_s", [128, 128], BF16))
        io["id_f"] = E(nc.sbuf_tensor("id_f", [128, 128], F32))
        io["ones_fs"] = E(nc.sbuf_tensor("ones_fsb", [1, 128], F32))
        io["write_inter"] = False
        io["d_ffn_out"] = {}
        io["d_l1_out"] = {}
        io["d_moe_out"] = {}
        io["eps_t"] = E(nc.sbuf_tensor("eps_t", [128, 1], F32))
        sc = Sched(nc, ctx, "f")
        sc.phase("pre")
        sc.op("vector", lambda q: q.memset(io["eps_t"][:], EPS))
        t1 = sc.dma_in("sync", io["id_f"][:], io["identf"][:])
        t2 = sc.dma_in("sync", io["ones_fs"][:], io["ones_f"][:])
        io["tok_pre"] = [t1, t2]
        sc.emit_ph()
        emit_l0_attn(sc, E, io)
        emit_l0_ffn(sc, E, io)
        emit_l1_attn(sc, E, io)
        emit_l1_moe(sc, E, io)
        sc.finalize()
    return nc


def host_fused_inputs(inputs):
    shared = {}
    shared.update(host_l0_ffn_inputs(inputs))
    shared.update(host_l1_attn_inputs(inputs))
    shared.update(host_l1_moe_inputs(inputs))
    shared["ones1"] = np.ones((1, 128), ml_dtypes.bfloat16)
    shared["wq"] = np.ascontiguousarray(np.asarray(inputs["L0_attn_w_q"], np.float32))
    shared["wk"] = np.ascontiguousarray(np.asarray(inputs["L0_attn_w_k"], np.float32))
    shared["bq"] = np.ascontiguousarray(np.asarray(inputs["L0_attn_b_q"], np.float32).reshape(QK, 1))
    shared["bk"] = np.ascontiguousarray(np.asarray(inputs["L0_attn_b_k"], np.float32).reshape(QK, 1))
    maskq = np.where(np.arange(128)[:, None] >= np.arange(128)[None, :], 0.0,
                     -40000.0).astype(np.float32)
    shared["maskq"] = maskq
    shared["maskT"] = np.ascontiguousarray(maskq.T)
    return shared



_CACHE = {}


def kernel(**inputs):
    x = np.asarray(inputs["L0_attn_x"], dtype=np.float32)
    if "fused" not in _CACHE:
        _CACHE["fused"] = build_fused()
    nc = _CACHE["fused"]
    shared = host_fused_inputs(inputs)
    in_maps = []
    for c in range(NCORES):
        d = dict(shared)
        d["x"] = np.ascontiguousarray(x[2 * c:2 * c + 2])
        in_maps.append(d)
    res = run_bass_kernel_spmd(nc, in_maps, core_ids=list(range(NCORES)))
    return np.concatenate([r["out"].reshape(NSEQ, S, D) for r in res.results], axis=0)
